# version 35 (final)
# baseline (speedup 1.0000x reference)
import math
from contextlib import ExitStack

import numpy as np
import ml_dtypes
import concourse.bass as bass
import concourse.mybir as mybir
from concourse.bass_utils import run_bass_kernel_spmd

F32 = mybir.dt.float32
BF16 = mybir.dt.bfloat16
I32 = mybir.dt.int32
AF = mybir.ActivationFunctionType
ALU = mybir.AluOpType
AXX = mybir.AxisListType.X

B, T, DM = 4, 2048, 2048
DL, DC, DFF = 512, 256, 5632
NCORES = 8
OWN = T // 2
LN_EPS = 1e-5
SCALE = 1.0 / 16.0
NEG = -1.0e30

NF4 = np.array([-1.0, -0.6962, -0.5251, -0.3949, -0.2844, -0.1848, -0.0911, 0.0,
                0.0796, 0.1609, 0.2461, 0.3379, 0.4407, 0.5626, 0.723, 1.0],
               dtype=ml_dtypes.bfloat16).astype(np.float32)
MID = ((NF4[:-1] + NF4[1:]) / 2.0).astype(np.float32)
DELTA = (NF4[1:] - NF4[:-1]).astype(np.float32)

TWO_PI = 2.0 * math.pi


def _split12(x):
    ui = np.float32(x).view(np.uint32) & np.uint32(0xFFFFF000)
    return float(ui.view(np.float32))


CW1 = _split12(TWO_PI)
CW2 = _split12(TWO_PI - CW1)
CW3 = float(np.float32(TWO_PI - CW1 - CW2))
INV2PI = float(np.float32(1.0 / TWO_PI))
PI_F = float(np.float32(math.pi))
PI_LO = float(np.nextafter(np.float32(math.pi), np.float32(0.0)))
PH1 = _split12(math.pi / 2)
PH2 = float(np.float32(math.pi / 2 - PH1))
LNTH = math.log(10000.0)

W12_ROWS = DFF // NCORES
W3_ROWS = DM // NCORES
NB_W12 = DM // 64
NB_W3 = DFF // 64
NSB_W12 = W12_ROWS * NB_W12 // 256
NSCAL_W3 = W3_ROWS * NB_W3
AG1_SHARD = NSCAL_W3 + 16
W12_RC = [128] * 5 + [64]


class Prog:
    def __init__(self, nc):
        self.nc = nc
        self.ops = []
        self.version = {}
        self.readers = {}
        self._floor = None

    @staticmethod
    def key(ap):
        if isinstance(ap, str):
            return ap
        t = getattr(ap, "tensor", ap)
        return t.name

    def _record(self, eng, kind, fn, reads, writes, dma_key=None, forced_inc=False):
        idx = len(self.ops)
        edges = set()
        if self._floor is not None:
            edges.add(self._floor)
        for ap in reads:
            w = self.version.get(self.key(ap))
            if w is not None:
                edges.add(w)
        for ap in writes:
            k = self.key(ap)
            w = self.version.get(k)
            if w is not None:
                edges.add(w)
            for r in self.readers.get(k, ()):
                edges.add(r)
        final_edges = set()
        for eidx in edges:
            if eidx == idx:
                continue
            if eng == "t" and self.ops[eidx]["eng"] == "t":
                continue
            final_edges.add(eidx)
        for ap in reads:
            self.readers.setdefault(self.key(ap), []).append(idx)
        for ap in writes:
            k = self.key(ap)
            self.version[k] = idx
            self.readers[k] = []
        self.ops.append(dict(eng=eng, kind=kind, fn=fn, edges=sorted(final_edges),
                             dma_key=dma_key, forced_inc=forced_inc))
        return idx

    def op(self, eng, fn, reads, writes, forced_inc=False):
        return self._record(eng, "op", fn, reads, writes, forced_inc=forced_inc)

    def dma(self, eng, dst, src, wkey=None, rkeys=None, **kw):
        def fn(e):
            return e.dma_start(out=dst, in_=src, **kw)
        wk = wkey if wkey is not None else self.key(dst)
        reads = rkeys if rkeys is not None else [src]
        return self._record(eng, "dma", fn, reads, [wk], dma_key=wk)

    def barrier(self):
        last = {}
        for i, o in enumerate(self.ops):
            if o["kind"] == "dma":
                last[("d", o["dma_key"])] = i
            else:
                last[o["eng"]] = i
        idx = len(self.ops)
        edges = sorted(v for v in last.values() if v != idx)
        self.ops.append(dict(eng="v", kind="op", fn=lambda e: e.nop(),
                             edges=edges, dma_key=None, forced_inc=False,
                             barrier=True))
        self._floor = idx
        self.version = {}
        self.readers = {}

    def finalize(self, block, sems_ctx):
        nc = self.nc
        needs_inc = [False] * len(self.ops)
        for o in self.ops:
            for e in o["edges"]:
                needs_inc[e] = True
        for i, o in enumerate(self.ops):
            if o["kind"] == "dma" or o["forced_inc"]:
                needs_inc[i] = True
        eng_sem = {}
        for eng in "vstg":
            eng_sem[eng] = sems_ctx.enter_context(nc.semaphore(f"se_{eng}"))
        sem_pool = []
        n_sems = 0
        key2sem = {}
        sem_count = {}
        eng_count = {e: 0 for e in "vstgy"}
        inc_info = [None] * len(self.ops)
        for i, o in enumerate(self.ops):
            if o.get("barrier"):
                for k in list(key2sem):
                    sem_pool.append(key2sem.pop(k))
            if o["kind"] == "dma":
                k = o["dma_key"]
                if k not in key2sem:
                    if sem_pool:
                        key2sem[k] = sem_pool.pop()
                    else:
                        h = sems_ctx.enter_context(nc.semaphore(f"sd_{n_sems}"))
                        n_sems += 1
                        sem_count[h.name] = 0
                        key2sem[k] = h
                h = key2sem[k]
                sem_count[h.name] += 16
                inc_info[i] = (h, sem_count[h.name], 16)
            elif needs_inc[i]:
                assert o["eng"] != "y"
                eng_count[o["eng"]] += 1
                inc_info[i] = (eng_sem[o["eng"]], eng_count[o["eng"]], 1)
        waited = {e: {} for e in "vstgy"}
        all_sems = {}
        for info in inc_info:
            if info is not None and info[2] == 16:
                all_sems[info[0].name] = (info[0], max(
                    all_sems.get(info[0].name, (None, 0))[1], info[1]))
        dma_final = [(v, h) for h, v in all_sems.values()]

        def emit_engine(eng, engobj):
            for i, o in enumerate(self.ops):
                if o["eng"] != eng:
                    continue
                for eidx in o["edges"]:
                    info = inc_info[eidx]
                    assert info is not None, (i, eidx)
                    sem, val, _ = info
                    if waited[eng].get(sem.name, -1) < val:
                        engobj.wait_ge(sem, val)
                        waited[eng][sem.name] = val
                ins = o["fn"](engobj)
                if inc_info[i] is not None:
                    sem, val, step = inc_info[i]
                    ins.then_inc(sem, step)

        @block.sync
        def _(e):
            emit_engine("y", e)
            for v, sem in dma_final:
                e.wait_ge(sem, v)

        @block.vector
        def _(e):
            emit_engine("v", e)

        @block.scalar
        def _(e):
            emit_engine("s", e)

        @block.tensor
        def _(e):
            emit_engine("t", e)

        @block.gpsimd
        def _(e):
            emit_engine("g", e)


def build_program():
    nc = bass.Bass("TRN2", target_bir_lowering=False, debug=False)
    x_full = nc.dram_tensor("x_full", [T, DM], F32, kind="ExternalInput").ap()
    x_own = nc.dram_tensor("x_own", [OWN, DM], F32, kind="ExternalInput").ap()
    rowbase = nc.dram_tensor("rowbase", [1, 1], F32, kind="ExternalInput").ap()
    oh_d = nc.dram_tensor("oh", [1, NCORES], F32, kind="ExternalInput").ap()
    wkv = nc.dram_tensor("wkv", [DL, DM], F32, kind="ExternalInput").ap()
    wcmp = nc.dram_tensor("wcmp", [DC, DL], F32, kind="ExternalInput").ap()
    gam = nc.dram_tensor("gam", [1, DC], F32, kind="ExternalInput").ap()
    bet = nc.dram_tensor("bet", [1, DC], F32, kind="ExternalInput").ap()
    wq = nc.dram_tensor("wq", [DC, DC], F32, kind="ExternalInput").ap()
    wkvc = nc.dram_tensor("wkvc", [2 * DC, DC], F32, kind="ExternalInput").ap()
    wc2l = nc.dram_tensor("wc2l", [DL, DC], F32, kind="ExternalInput").ap()
    wor = nc.dram_tensor("wor", [DM, DL], F32, kind="ExternalInput").ap()
    bor = nc.dram_tensor("bor", [1, DM], F32, kind="ExternalInput").ap()
    w1s = nc.dram_tensor("w1s", [W12_ROWS, DM], BF16, kind="ExternalInput").ap()
    w2s = nc.dram_tensor("w2s", [W12_ROWS, DM], BF16, kind="ExternalInput").ap()
    w3r = nc.dram_tensor("w3r", [W3_ROWS, DFF], BF16, kind="ExternalInput").ap()
    out = nc.dram_tensor("out", [OWN, DM], F32, kind="ExternalOutput").ap()

    s1w1_d = nc.dram_tensor("s1w1_d", [W12_ROWS * NB_W12], F32, kind="Internal").ap()
    s1w2_d = nc.dram_tensor("s1w2_d", [W12_ROWS * NB_W12], F32, kind="Internal").ap()
    s1w3_d = nc.dram_tensor("s1w3_d", [NSCAL_W3], F32, kind="Internal").ap()
    deq1_d = nc.dram_tensor("deq1_d", [W12_ROWS * NB_W12], BF16, kind="Internal").ap()
    deq2_d = nc.dram_tensor("deq2_d", [W12_ROWS * NB_W12], BF16, kind="Internal").ap()
    deq3_d = nc.dram_tensor("deq3_d", [NSCAL_W3], BF16, kind="Internal").ap()
    ag1_in = nc.dram_tensor("ag1_in", [AG1_SHARD], F32, kind="Internal").ap()
    ag1_out = nc.dram_tensor("ag1_out", [NCORES * AG1_SHARD], F32,
                             kind="Internal").ap()
    agw1_in = nc.dram_tensor("agw1_in", [DM, W12_ROWS], BF16, kind="Internal").ap()
    agw2_in = nc.dram_tensor("agw2_in", [DM, W12_ROWS], BF16, kind="Internal").ap()
    agw3_inA = nc.dram_tensor("agw3_inA", [DFF, 128], BF16,
                              kind="Internal").ap()
    agw3_inB = nc.dram_tensor("agw3_inB", [DFF, 128], BF16,
                              kind="Internal").ap()
    agw1_out = nc.dram_tensor("agw1_out", [NCORES, DM, W12_ROWS], BF16,
                              kind="Internal", addr_space="Shared").ap()
    agw2_out = nc.dram_tensor("agw2_out", [NCORES, DM, W12_ROWS], BF16,
                              kind="Internal", addr_space="Shared").ap()
    agw3_outA = nc.dram_tensor("agw3_outA", [NCORES, DFF, 128], BF16,
                               kind="Internal", addr_space="Shared").ap()
    agw3_outB = nc.dram_tensor("agw3_outB", [NCORES, DFF, 128], BF16,
                               kind="Internal", addr_space="Shared").ap()
    w3rm = nc.dram_tensor("w3rm", [DFF, DM], BF16, kind="Internal").ap()
    chx_in = nc.dram_tensor("chx_in", [2, 128, OWN], BF16,
                            kind="Internal").ap()
    chx_out = nc.dram_tensor("chx_out", [NCORES, 2, 128, OWN], BF16,
                             kind="Internal", addr_space="Shared").ap()
    mid_d = nc.dram_tensor("mid_d", [OWN, DM], F32, kind="Internal").ap()
    mb_d = nc.dram_tensor("mb_d", [1, 4], F32, kind="Internal").ap()

    p = Prog(nc)
    stack = []
    cms = {}

    def sb(name, shape, dtype):
        cm = nc.sbuf_tensor(name, shape, dtype)
        t = cm.__enter__()
        cms[name] = cm
        stack.append(name)
        return t

    def ps(name, shape, dtype=F32):
        cm = nc.psum_tensor(name, shape, dtype)
        t = cm.__enter__()
        cms[name] = cm
        return t

    def free(*names):
        ns = set(names)
        assert ns == set(stack[-len(ns):]), (
            f"non-suffix free: {sorted(ns)} vs {stack[-len(ns):]}")
        for n in reversed(stack[-len(ns):]):
            cms.pop(n).__exit__(None, None, None)
        del stack[-len(ns):]
        p.barrier()

    def free_ps(*names):
        for n in names:
            cms.pop(n).__exit__(None, None, None)
        p.barrier()

    ident_b = sb("ident_b", [128, 128], BF16)
    ones = sb("ones", [128, 1], F32)
    iota_p = sb("iota_p", [128, 1], F32)
    rb_bc = sb("rb_bc", [128, 1], F32)
    mscalar = sb("mscalar", [1, 4], F32)
    meanbc = sb("meanbc", [128, 3], F32)
    ln_mu = [sb(f"ln_mu{i}", [128, 1], F32) for i in range(2)]
    ln_ss = [sb(f"ln_ss{i}", [128, 1], F32) for i in range(2)]
    ln_rs = [sb(f"ln_rs{i}", [128, 1], F32) for i in range(2)]
    ln_nm = [sb(f"ln_nm{i}", [128, 1], F32) for i in range(2)]
    rowlim = sb("rowlim", [128, 1], F32)
    rowmax = sb("rowmax", [128, 1], F32)
    zsum = sb("zsum", [128, 1], F32)
    cen_buf = sb("cen_buf", [128, DM], F32)
    ff_f = sb("ff_f", [128, DM], F32)
    ln_b = sb("ln_b", [128, DM], BF16)
    xt_buf = [sb(f"xt_{i}", [128, DM], F32) for i in range(2)]
    ln1T = [sb(f"ln1T_{d}", [128, OWN], BF16) for d in range(16)]

    ps_tr = ps("ps_tr", [128, 128], BF16)
    ps_tr2 = ps("ps_tr2", [128, 128], BF16)
    ps_s1 = ps("ps_s1", [128, 512], F32)
    ps_s2 = ps("ps_s2", [128, 512], F32)
    ps_sc = ps("ps_sc", [128, 2048], F32)


    tr_ct = [0]

    def emit_tr(src_ap, src_t, dst_ap, dst_t, r=128, evac=None):
        pst = ps_tr if tr_ct[0] % 2 == 0 else ps_tr2
        eng = evac if evac is not None else ("v" if tr_ct[0] % 3 == 2 else "s")
        tr_ct[0] += 1
        p.op("t", lambda t, pst=pst, r=r: t.transpose(
            pst[:, 0:r], src_ap, ident_b[0:r, 0:r]),
            [src_t[:], ident_b[:]], [pst[:]])
        if eng == "v":
            p.op("v", lambda v, pst=pst, r=r: v.tensor_copy(
                out=dst_ap, in_=pst[:, 0:r]), [pst[:]], [dst_t[:]])
        else:
            p.op("s", lambda s, pst=pst, r=r: s.copy(
                out=dst_ap, in_=pst[:, 0:r]), [pst[:]], [dst_t[:]])

    ic = sb("ic", [128, 128], F32)
    ir = sb("ir", [128, 128], F32)
    p.op("g", lambda g: g.iota(ic[:], pattern=[[1, 128]], base=0,
         channel_multiplier=0, allow_small_or_imprecise_dtypes=True), [], [ic[:]])
    p.op("g", lambda g: g.iota(ir[:], pattern=[[0, 128]], base=0,
         channel_multiplier=1, allow_small_or_imprecise_dtypes=True), [], [ir[:]])
    p.op("v", lambda v: v.tensor_tensor(out=ident_b[:], in0=ic[:], in1=ir[:],
         op=ALU.is_equal), [ic[:], ir[:]], [ident_b[:]])
    p.op("v", lambda v: v.memset(ones[:], 1.0), [], [ones[:]])
    p.op("v", lambda v: v.tensor_copy(out=iota_p[:], in_=ir[:, 0:1]),
         [ir[:]], [iota_p[:]])
    p.dma("y", rb_bc[:], rowbase.broadcast_to([128, 1]))
    free("ic", "ir")

    tcols = sb("tcols", [128, T], F32)
    gam_bc = sb("gam_bc", [128, DC], BF16)
    bet_bc = sb("bet_bc", [128, DC], BF16)
    bor_bc = sb("bor_bc", [128, DM], F32)
    p.op("g", lambda g: g.iota(tcols[:], pattern=[[1, T]], base=0,
         channel_multiplier=0, allow_small_or_imprecise_dtypes=True),
         [], [tcols[:]])
    p.dma("y", ff_f[:, 0:DC], gam.broadcast_to([128, DC]))
    p.op("v", lambda v: v.tensor_copy(out=gam_bc[:], in_=ff_f[:, 0:DC]),
         [ff_f[:]], [gam_bc[:]])
    p.dma("y", ff_f[:, DC:2 * DC], bet.broadcast_to([128, DC]))
    p.op("v", lambda v: v.tensor_copy(out=bet_bc[:], in_=ff_f[:, DC:2 * DC]),
         [ff_f[:]], [bet_bc[:]])
    p.dma("y", bor_bc[:], bor.broadcast_to([128, DM]))


    dq_acc = sb("dq_acc", [128, 1024], F32)
    dq_term = sb("dq_term", [128, 1024], BF16)
    dq_stage = sb("dq_stage", [128, W12_ROWS], BF16)
    rowsum = sb("rowsum", [128, 1], F32)
    sq_cen = sb("sq_cen", [128, 256], F32)
    sq_sabs = sb("sq_sabs", [128, 1], F32)
    sq_qf = sb("sq_qf", [128, 1], F32)
    sq_rqf = sb("sq_rqf", [128, 1], F32)
    sq_a = sb("sq_a", [128, 256], F32)
    sq_i = sb("sq_i", [128, 256], I32)
    sq_b = sb("sq_b", [128, 256], F32)
    sq_bb = sb("sq_bb", [128, 256], BF16)
    sq_b2 = sb("sq_b2", [128, 256], BF16)

    def pass1(dram, shape_rows, ncols, nblk, s1_dst, sum_col, tag):
        off = 0
        i = 0
        while off < shape_rows:
            r = min(128, shape_rows - off)
            wt = sb(f"{tag}p1_{i}", [r, ncols], BF16)
            p.dma("y", wt[:], dram[off:off + r, :])
            p.op("v", lambda v, wt=wt, r=r, nblk=nblk: v.tensor_reduce(
                out=dq_acc[0:r, 0:nblk],
                in_=wt[:].rearrange("p (a b) -> p a b", b=64),
                axis=AXX, op=ALU.max, apply_absolute_value=True),
                [wt[:]], [dq_acc[:]])
            p.op("v", lambda v, r=r, nblk=nblk: v.tensor_reduce(
                out=rowsum[0:r, :], in_=dq_acc[0:r, 0:nblk], axis=AXX,
                op=ALU.add), [dq_acc[:]], [rowsum[:]])
            p.op("t", lambda t, r=r, f=(i == 0),
                 l=(off + r >= shape_rows): t.matmul(
                ps_s1[0:1, 0:1], lhsT=ones[0:r, :], rhs=rowsum[0:r, :],
                start=f, stop=l), [rowsum[:], ones[:]], [ps_s1[:]])
            p.dma("s", s1_dst.rearrange("(a b) -> a b", b=nblk)[off:off + r, :],
                  dq_acc[0:r, 0:nblk])
            off += r
            i += 1
        p.op("s", lambda s, c=sum_col: s.activation(
            out=mscalar[:, c:c + 1], in_=ps_s1[0:1, 0:1], func=AF.Copy),
            [ps_s1[:]], [mscalar[:]])
        free(*(f"{tag}p1_{j}" for j in range(i)))

    pass1(w1s, W12_ROWS, DM, NB_W12, s1w1_d, 0, "w1")
    pass1(w2s, W12_ROWS, DM, NB_W12, s1w2_d, 1, "w2")
    pass1(w3r, W3_ROWS, DFF, NB_W3, s1w3_d, 2, "w3")
    padz = sb("padz", [1, AG1_SHARD - NSCAL_W3], F32)
    p.op("v", lambda v: v.memset(padz[:], 0.0), [], [padz[:]])
    p.dma("s", ag1_in[0:NSCAL_W3].rearrange("(a b) -> a b", b=176),
          s1w3_d.rearrange("(a b) -> a b", b=176))
    p.dma("s", ag1_in[NSCAL_W3:AG1_SHARD].rearrange("(a b) -> a b", a=1), padz[:])
    p.dma("s", ag1_in[NSCAL_W3:NSCAL_W3 + 2].rearrange("(a b) -> a b", a=1),
          mscalar[0:1, 0:2])
    free("padz")
    p.op("g", lambda g: g.collective_compute(
        "AllGather", ALU.bypass, replica_groups=[list(range(NCORES))],
        ins=[ag1_in], outs=[ag1_out]), [ag1_in], [ag1_out], forced_inc=True)

    sumt = sb("sumt", [1, NCORES], F32)
    w3sc = sb("w3sc", [128, 176], F32)
    ag1v = ag1_out.rearrange("(k s) -> k s", s=AG1_SHARD)
    for j, total in ((0, DFF * NB_W12), (1, DFF * NB_W12)):
        p.dma("s", sumt[0:1, 0:NCORES], ag1v[:, NSCAL_W3 + j:NSCAL_W3 + j + 1],
              allow_slow_non_contiguous=True)
        p.op("v", lambda v: v.tensor_reduce(out=mscalar[0:1, 3:4], in_=sumt[:],
             axis=AXX, op=ALU.add), [sumt[:]], [mscalar[:]])
        p.op("s", lambda s, j=j, total=total: s.activation(
            out=mscalar[0:1, j:j + 1], in_=mscalar[0:1, 3:4], func=AF.Copy,
            scale=1.0 / total), [mscalar[:]], [mscalar[:]])
    for k in range(NCORES):
        p.dma("s", w3sc[:],
              ag1_out[k * AG1_SHARD:k * AG1_SHARD + NSCAL_W3]
              .rearrange("(a b) -> a b", b=176))
        p.op("v", lambda v: v.tensor_reduce(out=rowsum[:], in_=w3sc[:],
             axis=AXX, op=ALU.add), [w3sc[:]], [rowsum[:]])
        p.op("t", lambda t, f=(k == 0), l=(k == NCORES - 1): t.matmul(
            ps_s1[0:1, 0:1], lhsT=ones[:], rhs=rowsum[:], start=f, stop=l),
            [rowsum[:], ones[:]], [ps_s1[:]])
    p.op("s", lambda s: s.activation(out=mscalar[0:1, 2:3],
         in_=ps_s1[0:1, 0:1], func=AF.Copy, scale=1.0 / (DM * NB_W3)),
         [ps_s1[:]], [mscalar[:]])
    msc_b = sb("msc_b", [1, 4], BF16)
    p.op("s", lambda s_: s_.copy(out=msc_b[:], in_=mscalar[:]),
         [mscalar[:]], [msc_b[:]])
    p.op("s", lambda s_: s_.copy(out=mscalar[:], in_=msc_b[:]),
         [msc_b[:]], [mscalar[:]])
    p.dma("s", mb_d, mscalar[0:1, 0:4])
    p.dma("s", meanbc[:, 0:3], mb_d[0:1, 0:3].broadcast_to([128, 3]))
    free("msc_b")
    free("sumt", "w3sc")

    def scaler_quant(src_view, n_sb, mean_col, dst_view):
        p.dma("s", sq_cen[0:n_sb, :], src_view)
        p.op("v", lambda v: v.tensor_scalar(out=sq_bb[0:n_sb, :],
             in0=sq_cen[0:n_sb, :],
             scalar1=meanbc[0:n_sb, mean_col:mean_col + 1],
             scalar2=None, op0=ALU.subtract), [sq_cen[:], meanbc[:]],
             [sq_bb[:]])
        p.op("v", lambda v: v.tensor_copy(out=sq_cen[0:n_sb, :],
             in_=sq_bb[0:n_sb, :]), [sq_bb[:]], [sq_cen[:]])
        p.op("v", lambda v: v.tensor_reduce(out=sq_sabs[0:n_sb, :],
             in_=sq_cen[0:n_sb, :], axis=AXX, op=ALU.max,
             apply_absolute_value=True), [sq_cen[:]], [sq_sabs[:]])
        p.op("v", lambda v: v.reciprocal(out=sq_qf[0:n_sb, :],
             in_=sq_sabs[0:n_sb, :]), [sq_sabs[:]], [sq_qf[:]])
        p.op("v", lambda v: v.tensor_scalar(out=sq_bb[0:n_sb, 0:1],
             in0=sq_qf[0:n_sb, :], scalar1=128.0, scalar2=None, op0=ALU.mult),
             [sq_qf[:]], [sq_bb[:]])
        p.op("v", lambda v: v.tensor_copy(out=sq_qf[0:n_sb, :],
             in_=sq_bb[0:n_sb, 0:1]), [sq_bb[:]], [sq_qf[:]])
        p.op("v", lambda v: v.reciprocal(out=sq_rqf[0:n_sb, :],
             in_=sq_qf[0:n_sb, :]), [sq_qf[:]], [sq_rqf[:]])
        p.op("v", lambda v: v.tensor_scalar(out=sq_bb[0:n_sb, :],
             in0=sq_cen[0:n_sb, :], scalar1=sq_qf[0:n_sb, 0:1], scalar2=None,
             op0=ALU.mult), [sq_cen[:], sq_qf[:]], [sq_bb[:]])
        p.op("v", lambda v: v.tensor_copy(out=sq_a[0:n_sb, :],
             in_=sq_bb[0:n_sb, :]), [sq_bb[:]], [sq_a[:]])
        p.op("v", lambda v: v.tensor_copy(out=sq_i[0:n_sb, :],
             in_=sq_a[0:n_sb, :]), [sq_a[:]], [sq_i[:]])
        p.op("v", lambda v: v.tensor_copy(out=sq_cen[0:n_sb, :],
             in_=sq_i[0:n_sb, :]), [sq_i[:]], [sq_cen[:]])
        p.op("v", lambda v: v.tensor_tensor(out=sq_a[0:n_sb, :],
             in0=sq_a[0:n_sb, :], in1=sq_cen[0:n_sb, :], op=ALU.subtract),
             [sq_a[:], sq_cen[:]], [sq_a[:]])
        p.op("v", lambda v: v.tensor_scalar(out=sq_b[0:n_sb, :],
             in0=sq_a[0:n_sb, :], scalar1=0.5, scalar2=None, op0=ALU.is_gt),
             [sq_a[:]], [sq_b[:]])
        p.op("v", lambda v: v.tensor_tensor(out=sq_cen[0:n_sb, :],
             in0=sq_cen[0:n_sb, :], in1=sq_b[0:n_sb, :], op=ALU.add),
             [sq_cen[:], sq_b[:]], [sq_cen[:]])
        p.op("v", lambda v: v.tensor_scalar(out=sq_b[0:n_sb, :],
             in0=sq_a[0:n_sb, :], scalar1=-0.5, scalar2=None, op0=ALU.is_lt),
             [sq_a[:]], [sq_b[:]])
        p.op("v", lambda v: v.tensor_tensor(out=sq_cen[0:n_sb, :],
             in0=sq_cen[0:n_sb, :], in1=sq_b[0:n_sb, :], op=ALU.subtract),
             [sq_cen[:], sq_b[:]], [sq_cen[:]])
        p.op("v", lambda v: v.tensor_scalar(out=sq_cen[0:n_sb, :],
             in0=sq_cen[0:n_sb, :], scalar1=127.0, scalar2=-128.0,
             op0=ALU.min, op1=ALU.max), [sq_cen[:]], [sq_cen[:]])
        p.op("v", lambda v: v.tensor_scalar(out=sq_bb[0:n_sb, :],
             in0=sq_cen[0:n_sb, :], scalar1=sq_rqf[0:n_sb, 0:1], scalar2=None,
             op0=ALU.mult), [sq_cen[:], sq_rqf[:]], [sq_bb[:]])
        p.op("v", lambda v: v.tensor_scalar(out=sq_b2[0:n_sb, :],
             in0=sq_bb[0:n_sb, :],
             scalar1=meanbc[0:n_sb, mean_col:mean_col + 1],
             scalar2=None, op0=ALU.add), [sq_bb[:], meanbc[:]], [sq_b2[:]])
        p.dma("s", dst_view, sq_b2[0:n_sb, :])

    scaler_quant(s1w1_d.rearrange("(a b) -> a b", b=256), NSB_W12, 0,
                 deq1_d.rearrange("(a b) -> a b", b=256))
    scaler_quant(s1w2_d.rearrange("(a b) -> a b", b=256), NSB_W12, 1,
                 deq2_d.rearrange("(a b) -> a b", b=256))
    scaler_quant(s1w3_d.rearrange("(a b) -> a b", b=256), 88, 2,
                 deq3_d.rearrange("(a b) -> a b", b=256))

    def staircase(wt_ap, wt_t, rows, nblk, rs1_ap, rs1_t, deqt_ap, deqt_t,
                  wrt_ap, wrt_t):
        D = nblk * 64
        p.op("v", lambda v: v.tensor_tensor(
            out=wrt_ap.rearrange("p (a b) -> p a b", b=64),
            in0=wt_ap.rearrange("p (a b) -> p a b", b=64),
            in1=rs1_ap.rearrange("p (a o) -> p a o", o=1)
            .broadcast_to([rows, nblk, 64]),
            op=ALU.mult), [wt_t[:], rs1_t[:]], [wrt_t[:]])
        yield
        accB = ff_f[0:rows, 0:D]
        termB = ff_f[0:rows, 1024:1024 + D]
        p.op("v", lambda v: v.tensor_scalar(out=dq_acc[0:rows, 0:D],
             in0=wrt_ap, scalar1=float(MID[0]),
             scalar2=float(DELTA[0]), op0=ALU.is_gt, op1=ALU.mult),
             [wrt_t[:]], [dq_acc[:]])
        yield
        p.op("v", lambda v: v.tensor_scalar(out=accB,
             in0=wrt_ap, scalar1=float(MID[8]),
             scalar2=float(DELTA[8]), op0=ALU.is_gt, op1=ALU.mult),
             [wrt_t[:]], ["ffB"])
        p.op("v", lambda v: v.tensor_scalar(out=termB,
             in0=wrt_ap, scalar1=float(MID[9]),
             scalar2=float(DELTA[9]), op0=ALU.is_gt, op1=ALU.mult),
             [wrt_t[:]], ["ffT"])
        yield
        for i in range(1, 7):
            p.op("g", lambda g: g.tensor_tensor(out=accB,
                 in0=accB, in1=termB, op=ALU.add), ["ffB", "ffT"], ["ffB"])
            p.op("v", lambda v, j=i: v.tensor_scalar(out=dq_term[0:rows, 0:D],
                 in0=wrt_ap, scalar1=float(MID[j]),
                 scalar2=float(DELTA[j]), op0=ALU.is_gt, op1=ALU.mult),
                 [wrt_t[:]], [dq_term[:]])
            p.op("v", lambda v: v.tensor_tensor(out=dq_acc[0:rows, 0:D],
                 in0=dq_acc[0:rows, 0:D], in1=dq_term[0:rows, 0:D],
                 op=ALU.add), [dq_acc[:], dq_term[:]], [dq_acc[:]])
            if i < 6:
                p.op("v", lambda v, j=9 + i: v.tensor_scalar(out=termB,
                     in0=wrt_ap, scalar1=float(MID[j]),
                     scalar2=float(DELTA[j]), op0=ALU.is_gt, op1=ALU.mult),
                     [wrt_t[:]], ["ffT"])
            yield
        p.op("v", lambda v: v.tensor_scalar(out=dq_term[0:rows, 0:D],
             in0=wrt_ap, scalar1=float(MID[7]),
             scalar2=float(DELTA[7]), op0=ALU.is_gt, op1=ALU.mult),
             [wrt_t[:]], [dq_term[:]])
        p.op("v", lambda v: v.tensor_tensor(out=dq_acc[0:rows, 0:D],
             in0=dq_acc[0:rows, 0:D], in1=dq_term[0:rows, 0:D],
             op=ALU.add), [dq_acc[:], dq_term[:]], [dq_acc[:]])
        yield
        p.op("v", lambda v: v.tensor_tensor(out=dq_acc[0:rows, 0:D],
             in0=dq_acc[0:rows, 0:D], in1=accB, op=ALU.add),
             [dq_acc[:], "ffB"], [dq_acc[:]])
        p.op("v", lambda v: v.scalar_tensor_tensor(
            out=wrt_ap.rearrange("p (a b) -> p a b", b=64),
            in0=dq_acc[0:rows, 0:D].rearrange("p (a b) -> p a b", b=64),
            scalar=1.0,
            in1=deqt_ap.rearrange("p (a o) -> p a o", o=1)
            .broadcast_to([rows, nblk, 64]),
            op0=ALU.subtract, op1=ALU.mult), [dq_acc[:], deqt_t[:]], [wrt_t[:]])


    free("sq_b2", "sq_bb", "sq_b", "sq_i", "sq_a", "sq_rqf",
         "sq_qf", "sq_sabs", "sq_cen", "rowsum", "dq_stage")
    qw_raw = sb("qw_raw", [128, DM], BF16)
    qw_wrt = sb("qw_wrt", [128, DM], BF16)
    qw_rs1 = sb("qw_rs1", [128, NB_W12], F32)
    qw_deqt = sb("qw_deqt", [128, NB_W12], BF16)
    qstg = [sb(f"qstg_{i}", [128, 128], BF16) for i in range(2)]

    def w12_dequant_queue():
        ags = []
        for dram, s1_d, deq_d, ag_in, ag_out_t in (
                (w1s, s1w1_d, deq1_d, agw1_in, agw1_out),
                (w2s, s1w2_d, deq2_d, agw2_in, agw2_out)):
            off = 0
            for i, r in enumerate(W12_RC):
                p.dma("y", qw_raw[0:r, :], dram[off:off + r, :])
                p.dma("y", qw_rs1[0:r, :],
                      s1_d.rearrange("(a b) -> a b", b=NB_W12)[off:off + r, :])
                p.op("v", lambda v, r=r: v.reciprocal(out=qw_rs1[0:r, :],
                     in_=qw_rs1[0:r, :]), [qw_rs1[:]], [qw_rs1[:]])
                p.dma("y", qw_deqt[0:r, :],
                      deq_d.rearrange("(a b) -> a b", b=NB_W12)[off:off + r, :])
                for b0 in (0, 16):
                    yield from staircase(
                        qw_raw[0:r, b0 * 64:(b0 + 16) * 64], qw_raw, r,
                        16, qw_rs1[0:r, b0:b0 + 16], qw_rs1,
                        qw_deqt[0:r, b0:b0 + 16], qw_deqt,
                        qw_wrt[0:r, b0 * 64:(b0 + 16) * 64], qw_wrt)
                    yield
                r0 = sum(W12_RC[:i])
                for dc in range(DM // 128):
                    stg = qstg[dc % 2]
                    emit_tr(qw_wrt[0:r, dc * 128:(dc + 1) * 128], qw_wrt,
                            stg[:, 0:r], stg, r=r, evac="s")
                    p.dma("y", ag_in[dc * 128:(dc + 1) * 128, r0:r0 + r],
                          stg[:, 0:r], wkey=f"qag{dc % 4}",
                          allow_slow_non_contiguous=True)
                    if dc % 2 == 1:
                        yield
                off += r
            ags.append((ag_in, ag_out_t))
        for ag_in, ag_out_t in ags:
            p.op("g", lambda g, ag_in=ag_in, ag_out_t=ag_out_t:
                 g.collective_compute(
                "AllGather", ALU.bypass, replica_groups=[list(range(NCORES))],
                ins=[ag_in], outs=[ag_out_t]),
                [ag_in] + [f"qag{j}" for j in range(4)], [ag_out_t],
                forced_inc=True)
            yield

    w12q = w12_dequant_queue()

    def drain_w12(n):
        for _ in range(n):
            try:
                next(w12q)
            except StopIteration:
                return


    wkv_nat = [sb(f"wkvn_{i}", [128, DM], BF16) for i in range(4)]
    worT = [sb(f"worT_{i}", [128, DM], BF16) for i in range(4)]
    wc2lT = [sb(f"wc2lT_{i}", [128, DL], BF16) for i in range(2)]
    wkvcT = [sb(f"wkvcT_{i}", [128, 2 * DC], BF16) for i in range(2)]
    wqT = [sb(f"wqT_{i}", [128, DC], BF16) for i in range(2)]
    chT_full = [sb(f"chTf_{d}", [128, T], BF16) for d in range(2)]
    chT_own = [sb(f"chTo_{d}", [128, OWN], BF16) for d in range(2)]
    sinF = sb("sinF", [128, T], BF16)
    cosF = sb("cosF", [128, T], BF16)
    sinQ = sb("sinQ", [128, OWN], BF16)
    cosQ = sb("cosQ", [128, OWN], BF16)
    wcmpT = [sb(f"wcmpT_{i}", [128, DC], BF16) for i in range(4)]
    wkvT = [sb(f"wkvT_{i}", [128, DL], BF16) for i in range(16)]
    wld = ff_f
    wld_b = ln_b

    def load_transpose(dram, rows_total, cols, outs, nat_tiles=None):
        r0 = 0
        for i in range((rows_total + 127) // 128):
            r = min(128, rows_total - i * 128)
            p.dma("y", wld[0:r, 0:cols], dram[i * 128:i * 128 + r, :])
            p.op("s", lambda s, r=r, c=cols: s.copy(
                out=wld_b[0:r, 0:c], in_=wld[0:r, 0:c]), [wld[:]], [wld_b[:]])
            if nat_tiles is not None:
                p.op("v", lambda v, i=i, r=r, c=cols: v.tensor_copy(
                    out=nat_tiles[i][:], in_=wld_b[0:r, 0:c]),
                    [wld_b[:]], [nat_tiles[i][:]])
            for c in range(cols // 128):
                emit_tr(wld_b[0:r, c * 128:(c + 1) * 128], wld_b,
                        outs[c][:, r0:r0 + r], outs[c], r=r)
            r0 += r

    load_transpose(wkv, DL, DM, wkvT, nat_tiles=wkv_nat)
    load_transpose(wcmp, DC, DL, wcmpT)
    load_transpose(wq, DC, DC, wqT)
    load_transpose(wkvc, 2 * DC, DC, wkvcT)
    load_transpose(wc2l, DL, DC, wc2lT)
    load_transpose(wor, DM, DL, worT)

    invf = sb("invf", [128, 1], F32)
    ropi = sb("ropi", [128, 1024], I32)
    rop1 = cen_buf
    rop2 = ff_f
    rop3 = xt_buf[0]

    p.op("v", lambda v: v.tensor_scalar(out=invf[:], in0=iota_p[:],
         scalar1=float(-LNTH / 128.0), scalar2=None, op0=ALU.mult),
         [iota_p[:]], [invf[:]])
    p.op("s", lambda s: s.activation(out=invf[:], in_=invf[:], func=AF.Exp),
         [invf[:]], [invf[:]])

    def rope_tables(n, sin_t, cos_t, add_rb):
        if add_rb:
            p.op("v", lambda v: v.tensor_scalar(out=rop1[:, 0:n],
                 in0=tcols[:, 0:n], scalar1=rb_bc[:, 0:1], scalar2=invf[:, 0:1],
                 op0=ALU.add, op1=ALU.mult), [tcols[:], rb_bc[:], invf[:]],
                 [rop1[:]])
        else:
            p.op("v", lambda v: v.tensor_scalar(out=rop1[:, 0:n],
                 in0=tcols[:, 0:n], scalar1=invf[:, 0:1], scalar2=None,
                 op0=ALU.mult), [tcols[:], invf[:]], [rop1[:]])
        p.op("v", lambda v: v.tensor_scalar(out=rop2[:, 0:n], in0=rop1[:, 0:n],
             scalar1=INV2PI, scalar2=None, op0=ALU.mult), [rop1[:]], [rop2[:]])
        for c0 in range(0, n, 1024):
            cw = min(1024, n - c0)
            p.op("v", lambda v, c0=c0, cw=cw: v.tensor_copy(
                out=ropi[:, 0:cw], in_=rop2[:, c0:c0 + cw]),
                [rop2[:]], [ropi[:]])
            p.op("v", lambda v, c0=c0, cw=cw: v.tensor_copy(
                out=rop3[:, c0:c0 + cw], in_=ropi[:, 0:cw]),
                [ropi[:]], [rop3[:]])
        p.op("v", lambda v: v.tensor_tensor(out=rop2[:, 0:n], in0=rop2[:, 0:n],
             in1=rop3[:, 0:n], op=ALU.subtract), [rop2[:], rop3[:]], [rop2[:]])
        p.op("v", lambda v: v.tensor_scalar(out=ln_b[:, 0:n], in0=rop2[:, 0:n],
             scalar1=0.5, scalar2=None, op0=ALU.is_gt), [rop2[:]], [ln_b[:]])
        p.op("v", lambda v: v.tensor_tensor(out=rop3[:, 0:n], in0=rop3[:, 0:n],
             in1=ln_b[:, 0:n], op=ALU.add), [rop3[:], ln_b[:]], [rop3[:]])
        p.op("v", lambda v: v.tensor_scalar(out=ln_b[:, 0:n], in0=rop2[:, 0:n],
             scalar1=-0.5, scalar2=None, op0=ALU.is_lt), [rop2[:]], [ln_b[:]])
        p.op("v", lambda v: v.tensor_tensor(out=rop3[:, 0:n], in0=rop3[:, 0:n],
             in1=ln_b[:, 0:n], op=ALU.subtract), [rop3[:], ln_b[:]], [rop3[:]])
        p.op("v", lambda v: v.scalar_tensor_tensor(out=rop2[:, 0:n],
             in0=rop3[:, 0:n], scalar=-CW1, in1=rop1[:, 0:n],
             op0=ALU.mult, op1=ALU.add), [rop3[:], rop1[:]], [rop2[:]])
        p.op("v", lambda v: v.scalar_tensor_tensor(out=rop2[:, 0:n],
             in0=rop3[:, 0:n], scalar=-CW2, in1=rop2[:, 0:n],
             op0=ALU.mult, op1=ALU.add), [rop3[:], rop2[:]], [rop2[:]])
        p.op("v", lambda v: v.scalar_tensor_tensor(out=rop2[:, 0:n],
             in0=rop3[:, 0:n], scalar=-CW3, in1=rop2[:, 0:n],
             op0=ALU.mult, op1=ALU.add), [rop3[:], rop2[:]], [rop2[:]])
        p.op("v", lambda v: v.tensor_scalar(out=rop2[:, 0:n], in0=rop2[:, 0:n],
             scalar1=PI_LO, scalar2=-PI_LO, op0=ALU.min, op1=ALU.max),
             [rop2[:]], [rop2[:]])
        p.op("s", lambda s: s.activation(out=sin_t[:, 0:n], in_=rop2[:, 0:n],
             func=AF.Sin), [rop2[:]], [sin_t[:]])
        p.op("v", lambda v: v.tensor_scalar(out=rop1[:, 0:n], in0=rop2[:, 0:n],
             scalar1=PH1, scalar2=PH2, op0=ALU.add, op1=ALU.add),
             [rop2[:]], [rop1[:]])
        p.op("v", lambda v: v.tensor_scalar(out=rop2[:, 0:n], in0=rop1[:, 0:n],
             scalar1=PI_F, scalar2=None, op0=ALU.is_gt), [rop1[:]], [rop2[:]])
        p.op("v", lambda v: v.scalar_tensor_tensor(out=rop1[:, 0:n],
             in0=rop2[:, 0:n], scalar=-CW1, in1=rop1[:, 0:n],
             op0=ALU.mult, op1=ALU.add), [rop2[:], rop1[:]], [rop1[:]])
        p.op("v", lambda v: v.scalar_tensor_tensor(out=rop1[:, 0:n],
             in0=rop2[:, 0:n], scalar=-(CW2 + CW3), in1=rop1[:, 0:n],
             op0=ALU.mult, op1=ALU.add), [rop2[:], rop1[:]], [rop1[:]])
        p.op("v", lambda v: v.tensor_scalar(out=rop1[:, 0:n], in0=rop1[:, 0:n],
             scalar1=PI_LO, scalar2=-PI_LO, op0=ALU.min, op1=ALU.max),
             [rop1[:]], [rop1[:]])
        p.op("s", lambda s: s.activation(out=cos_t[:, 0:n], in_=rop1[:, 0:n],
             func=AF.Sin), [rop1[:]], [cos_t[:]])

    rope_tables(T, sinF, cosF, add_rb=False)
    rope_tables(OWN, sinQ, cosQ, add_rb=True)
    free("invf", "ropi")


    def layernorm_to(src_ap, src_t, rows, cols, out_ap, out_t, si=0,
                     act_apply=False):
        mu, ss, rs = ln_mu[si], ln_ss[si], ln_rs[si]
        p.op("s", lambda s: s.activation(out=cen_buf[0:rows, 0:cols]
             .bitcast(F32) if False else cen_buf[0:rows, 0:cols],
             in_=src_ap, func=AF.Square, accum_out=ss[0:rows, :]),
             [src_t[:]], [cen_buf[:], ss[:]])
        p.op("v", lambda v: v.tensor_reduce(out=mu[0:rows, :], in_=src_ap,
             axis=AXX, op=ALU.add), [src_t[:]], [mu[:]])
        p.op("v", lambda v: v.tensor_scalar(out=mu[0:rows, :],
             in0=mu[0:rows, :], scalar1=1.0 / cols, scalar2=None,
             op0=ALU.mult), [mu[:]], [mu[:]])
        p.op("v", lambda v: v.tensor_tensor(out=rs[0:rows, :],
             in0=mu[0:rows, :], in1=mu[0:rows, :], op=ALU.mult),
             [mu[:]], [rs[:]])
        p.op("v", lambda v: v.tensor_scalar(out=ss[0:rows, :],
             in0=ss[0:rows, :], scalar1=1.0 / cols, scalar2=LN_EPS,
             op0=ALU.mult, op1=ALU.add), [ss[:]], [ss[:]])
        p.op("v", lambda v: v.tensor_tensor(out=ss[0:rows, :],
             in0=ss[0:rows, :], in1=rs[0:rows, :], op=ALU.subtract),
             [ss[:], rs[:]], [ss[:]])
        p.op("s", lambda s: s.activation(out=ss[0:rows, :],
             in_=ss[0:rows, :], func=AF.Sqrt), [ss[:]], [ss[:]])
        p.op("v", lambda v: v.reciprocal(out=rs[0:rows, :],
             in_=ss[0:rows, :]), [ss[:]], [rs[:]])
        if act_apply:
            nm = ln_nm[si]
            p.op("v", lambda v: v.tensor_scalar(out=nm[0:rows, :],
                 in0=mu[0:rows, :], scalar1=rs[0:rows, 0:1], scalar2=-1.0,
                 op0=ALU.mult, op1=ALU.mult), [mu[:], rs[:]], [nm[:]])
            p.op("s", lambda s: s.activation(out=out_ap, in_=src_ap,
                 func=AF.Identity, bias=nm[0:rows, 0:1],
                 scale=rs[0:rows, 0:1]),
                 [src_t[:], nm[:], rs[:]], [out_t[:]])
        else:
            p.op("v", lambda v: v.tensor_scalar(out=out_ap,
                 in0=src_ap, scalar1=mu[0:rows, 0:1], scalar2=rs[0:rows, 0:1],
                 op0=ALU.subtract, op1=ALU.mult),
                 [src_t[:], mu[:], rs[:]], [out_t[:]])

    def token_chain(x_dram, ntok, chT_tiles, tag):
        ln0T = [sb(f"{tag}l_{d}", [128, 512], BF16) for d in range(16)]
        kvf = [sb(f"{tag}k_{m}", [128, 512], BF16) for m in range(4)]
        ch_b = sb(f"{tag}c", [128, DC], BF16)
        cf = sb(f"{tag}f", [128, DC], F32)
        for tb in range(ntok // 512):
            for i in range(4):
                ti = tb * 4 + i
                drain_w12(2)
                xt = xt_buf[ti % 2]
                p.dma("y", xt[:], x_dram[ti * 128:(ti + 1) * 128, :])
                layernorm_to(xt[:], xt, 128, DM, ln_b[:], ln_b, si=ti % 2,
                             act_apply=True)
                for d in range(16):
                    emit_tr(ln_b[:, d * 128:(d + 1) * 128], ln_b,
                            ln0T[d][:, i * 128:(i + 1) * 128], ln0T[d])
                    drain_w12(1)
            for m in range(4):
                for d in range(16):
                    p.op("t", lambda t, m=m, d=d: t.matmul(
                        ps_s1[:], lhsT=wkvT[d][:, m * 128:(m + 1) * 128],
                        rhs=ln0T[d][:, 0:512],
                        start=(d == 0), stop=(d == 15)),
                        [wkvT[d][:], ln0T[d][:]], [ps_s1[:]])
                p.op("s", lambda s, m=m: s.copy(out=kvf[m][:], in_=ps_s1[:]),
                     [ps_s1[:]], [kvf[m][:]])
                drain_w12(1)
            for i in range(4):
                ti = tb * 4 + i
                drain_w12(1)
                for m in range(4):
                    p.op("t", lambda t, m=m, i=i: t.matmul(
                        ps_s1[:, 0:DC], lhsT=kvf[m][:, i * 128:(i + 1) * 128],
                        rhs=wcmpT[m][:], start=(m == 0), stop=(m == 3)),
                        [kvf[m][:], wcmpT[m][:]], [ps_s1[:]])
                p.op("v", lambda v: v.tensor_copy(out=cf[:], in_=ps_s1[:, 0:DC]),
                     [ps_s1[:]], [cf[:]])
                layernorm_to(cf[:], cf, 128, DC, cf[:], cf, si=i % 2)
                drain_w12(1)
                p.op("v", lambda v: v.tensor_tensor(out=cf[:], in0=cf[:],
                     in1=gam_bc[:, 0:DC], op=ALU.mult), [cf[:], gam_bc[:]],
                     [cf[:]])
                p.op("v", lambda v: v.tensor_tensor(out=ch_b[:], in0=cf[:],
                     in1=bet_bc[:, 0:DC], op=ALU.add), [cf[:], bet_bc[:]],
                     [ch_b[:]])
                for d in range(2):
                    emit_tr(ch_b[:, d * 128:(d + 1) * 128], ch_b,
                            chT_tiles[d][:, ti * 128:(ti + 1) * 128],
                            chT_tiles[d])
        free(f"{tag}f", f"{tag}c", *(f"{tag}k_{m}" for m in range(3, -1, -1)),
             *(f"{tag}l_{d}" for d in range(15, -1, -1)))

    token_chain(x_own, OWN, chT_own, "Bq")
    for d in range(2):
        p.dma("y", chx_in[d], chT_own[d][:])
    p.op("g", lambda g: g.collective_compute(
        "AllGather", ALU.bypass, replica_groups=[list(range(NCORES))],
        ins=[chx_in], outs=[chx_out]), [chx_in], [chx_out], forced_inc=True)
    ohbc = sb("ohbc", [128, NCORES], F32)
    p.dma("y", ohbc[:], oh_d.broadcast_to([128, NCORES]))
    exst = [ln_b[:, 0:OWN], ln_b[:, OWN:2 * OWN]]
    for j in range(NCORES):
        hh = j % 2
        for d in range(2):
            sk = f"ex{(2 * j + d) % 2}"
            p.dma("y" if (j + d) % 2 == 0 else "s",
                  exst[(2 * j + d) % 2], chx_out[j, d],
                  wkey=sk, rkeys=[chx_out])
            dst = chT_full[d][:, hh * OWN:(hh + 1) * OWN]
            if j < 2:
                p.op("v", lambda v, dst=dst, sk=sk, j=j,
                     src=exst[(2 * j + d) % 2]: v.tensor_scalar(
                    out=dst, in0=src, scalar1=ohbc[:, j:j + 1],
                    scalar2=None, op0=ALU.mult),
                    [sk, ohbc[:]], [chT_full[d][:]])
            else:
                p.op("v", lambda v, dst=dst, sk=sk, j=j,
                     src=exst[(2 * j + d) % 2]: v.scalar_tensor_tensor(
                    out=dst, in0=src, scalar=ohbc[:, j:j + 1],
                    in1=dst, op0=ALU.mult, op1=ALU.add),
                    [sk, ohbc[:], chT_full[d][:]], [chT_full[d][:]])
    free("ohbc")
    drain_w12(4)
    free(*(f"wkvT_{i}" for i in range(16)), *(f"wcmpT_{i}" for i in range(4)))
    kf_b = [sb(f"kfb_{m}", [128, T], BF16) for m in range(2)]
    qf_b = [sb(f"qfb_{m}", [128, OWN], BF16) for m in range(2)]
    v_km = [sb(f"vkm_{i}", [128, 256], BF16) for i in range(16)]

    kpre = [sb(f"kpre_{m}", [128, T], BF16) for m in range(2)]
    vpre = [sb(f"vpre_{m}", [128, T], BF16) for m in range(2)]
    qpre = [sb(f"qpre_{m}", [128, OWN], BF16) for m in range(2)]
    for tb in range(T // 512):
        for m in range(4):
            for d in range(2):
                p.op("t", lambda t, m=m, d=d, tb=tb: t.matmul(
                    ps_s1[:], lhsT=wkvcT[d][:, m * 128:(m + 1) * 128],
                    rhs=chT_full[d][:, tb * 512:(tb + 1) * 512],
                    start=(d == 0), stop=(d == 1)),
                    [wkvcT[d][:], chT_full[d][:]], [ps_s1[:]])
            dstl = kpre if m < 2 else vpre
            p.op("s", lambda s, m=m, tb=tb, dstl=dstl: s.copy(
                out=dstl[m % 2][:, tb * 512:(tb + 1) * 512], in_=ps_s1[:]),
                [ps_s1[:]], [dstl[m % 2][:]])
            drain_w12(2)
    for tb in range(OWN // 512):
        for m in range(2):
            for d in range(2):
                p.op("t", lambda t, m=m, d=d, tb=tb: t.matmul(
                    ps_s1[:], lhsT=wqT[d][:, m * 128:(m + 1) * 128],
                    rhs=chT_own[d][:, tb * 512:(tb + 1) * 512],
                    start=(d == 0), stop=(d == 1)),
                    [wqT[d][:], chT_own[d][:]], [ps_s1[:]])
            p.op("s", lambda s, m=m, tb=tb: s.copy(
                out=qpre[m][:, tb * 512:(tb + 1) * 512], in_=ps_s1[:]),
                [ps_s1[:]], [qpre[m][:]])
            drain_w12(2)

    def rope_apply(pre, post, sin_t, cos_t, n):
        p.op("v", lambda v: v.tensor_tensor(out=cen_buf[:, 0:n], in0=pre[1][:],
             in1=sin_t[:, 0:n], op=ALU.mult), [pre[1][:], sin_t[:]],
             [cen_buf[:]])
        p.op("v", lambda v: v.tensor_tensor(out=post[0][:], in0=pre[0][:],
             in1=cos_t[:, 0:n], op=ALU.mult), [pre[0][:], cos_t[:]],
             [post[0][:]])
        p.op("v", lambda v: v.tensor_tensor(out=post[0][:], in0=post[0][:],
             in1=cen_buf[:, 0:n], op=ALU.subtract),
             [post[0][:], cen_buf[:]], [post[0][:]])
        p.op("v", lambda v: v.tensor_tensor(out=cen_buf[:, 0:n], in0=pre[0][:],
             in1=sin_t[:, 0:n], op=ALU.mult), [pre[0][:], sin_t[:]],
             [cen_buf[:]])
        p.op("v", lambda v: v.tensor_tensor(out=post[1][:], in0=pre[1][:],
             in1=cos_t[:, 0:n], op=ALU.mult), [pre[1][:], cos_t[:]],
             [post[1][:]])
        p.op("v", lambda v: v.tensor_tensor(out=post[1][:], in0=post[1][:],
             in1=cen_buf[:, 0:n], op=ALU.add),
             [post[1][:], cen_buf[:]], [post[1][:]])

    rope_apply(kpre, kf_b, sinF, cosF, T)
    rope_apply(qpre, qf_b, sinQ, cosQ, OWN)
    for i in range(16):
        for d in range(2):
            emit_tr(vpre[d][:, i * 128:(i + 1) * 128], vpre[d],
                    v_km[i][:, d * 128:(d + 1) * 128], v_km[i])
    free("qpre_1", "qpre_0", "vpre_1", "vpre_0", "kpre_1", "kpre_0")

    of_b = [sb(f"of_{d}", [128, OWN], BF16) for d in range(2)]
    scm = cen_buf
    prob_b = [sb(f"prob_b{i}", [128, 2048], BF16) for i in range(2)]
    probT = [sb(f"probT{i}", [128, 2048], BF16) for i in range(2)]

    for t_i in range(8):
        kmax = 1152 + 128 * t_i
        nkb = (kmax + 511) // 512
        for kb in range(nkb):
            nk = min(512, kmax - kb * 512)
            for d in range(2):
                p.op("t", lambda t, t_i=t_i, kb=kb, nk=nk, d=d: t.matmul(
                    ps_sc[:, kb * 512:kb * 512 + nk],
                    lhsT=qf_b[d][:, t_i * 128:(t_i + 1) * 128],
                    rhs=kf_b[d][:, kb * 512:kb * 512 + nk],
                    start=(d == 0), stop=(d == 1)),
                    [qf_b[d][:], kf_b[d][:]], [ps_sc[:]])
        drain_w12(2)
        p.op("v", lambda v, t_i=t_i: v.tensor_scalar(out=rowlim[:],
             in0=iota_p[:], scalar1=float(128 * t_i + 1), scalar2=None,
             op0=ALU.add), [iota_p[:]], [rowlim[:]])
        p.op("v", lambda v: v.tensor_tensor(out=rowlim[:], in0=rowlim[:],
             in1=rb_bc[:], op=ALU.add), [rowlim[:], rb_bc[:]], [rowlim[:]])
        p.op("v", lambda v, kmax=kmax: v.tensor_scalar(out=scm[:, 0:kmax],
             in0=tcols[:, 0:kmax], scalar1=rowlim[:, 0:1], scalar2=NEG,
             op0=ALU.is_ge, op1=ALU.mult), [tcols[:], rowlim[:]], [scm[:]])
        p.op("v", lambda v, kmax=kmax: v.tensor_tensor(out=scm[:, 0:kmax],
             in0=scm[:, 0:kmax], in1=ps_sc[:, 0:kmax], op=ALU.add),
             [scm[:], ps_sc[:]], [scm[:]])
        p.op("v", lambda v, kmax=kmax: v.tensor_reduce(out=rowmax[:],
             in_=scm[:, 0:kmax], axis=AXX, op=ALU.max), [scm[:]], [rowmax[:]])
        p.op("v", lambda v: v.tensor_scalar(out=rowmax[:], in0=rowmax[:],
             scalar1=-SCALE, scalar2=None, op0=ALU.mult), [rowmax[:]],
             [rowmax[:]])
        pb = prob_b[t_i % 2]
        pT = probT[t_i % 2]
        p.op("s", lambda s, kmax=kmax, pb=pb: s.activation(out=pb[:, 0:kmax],
             in_=scm[:, 0:kmax], func=AF.Exp, bias=rowmax[:, 0:1], scale=SCALE,
             accum_out=zsum[:]), [scm[:], rowmax[:]], [pb[:], zsum[:]])
        p.op("v", lambda v: v.reciprocal(out=zsum[:], in_=zsum[:]),
             [zsum[:]], [zsum[:]])
        p.op("s", lambda s, kmax=kmax, pb=pb: s.activation(
             out=pb[:, 0:kmax], in_=pb[:, 0:kmax], func=AF.Copy,
             bias=0.0, scale=zsum[:, 0:1]), [pb[:], zsum[:]], [pb[:]])
        nkc = kmax // 128
        for kc in range(nkc):
            emit_tr(pb[:, kc * 128:(kc + 1) * 128], pb,
                    pT[:, kc * 128:(kc + 1) * 128], pT)
            if kc % 2 == 1:
                drain_w12(1)
        for d in range(2):
            for kc in range(nkc):
                p.op("t", lambda t, d=d, kc=kc, nkc=nkc, pT=pT: t.matmul(
                    ps_s1[:, 0:128], lhsT=v_km[kc][:, d * 128:(d + 1) * 128],
                    rhs=pT[:, kc * 128:(kc + 1) * 128],
                    start=(kc == 0), stop=(kc == nkc - 1)),
                    [v_km[kc][:], pT[:]], [ps_s1[:]])
            p.op("s", lambda s, d=d, t_i=t_i: s.copy(
                out=of_b[d][:, t_i * 128:(t_i + 1) * 128],
                in_=ps_s1[:, 0:128]), [ps_s1[:]], [of_b[d][:]])
    free("probT1", "probT0", "prob_b1", "prob_b0")

    zf_b = [sb(f"zf_{m}", [128, OWN], BF16) for m in range(4)]
    for tb in range(OWN // 512):
        for m in range(4):
            for d in range(2):
                p.op("t", lambda t, m=m, d=d, tb=tb: t.matmul(
                    ps_s1[:], lhsT=wc2lT[d][:, m * 128:(m + 1) * 128],
                    rhs=of_b[d][:, tb * 512:(tb + 1) * 512],
                    start=(d == 0), stop=(d == 1)),
                    [wc2lT[d][:], of_b[d][:]], [ps_s1[:]])
            p.op("s", lambda s, m=m, tb=tb: s.copy(
                out=zf_b[m][:, tb * 512:(tb + 1) * 512], in_=ps_s1[:]),
                [ps_s1[:]], [zf_b[m][:]])

    ao_t = sb("ao_t", [128, DM], F32)
    for ti in range(8):
        for nb in range(4):
            for m in range(4):
                p.op("t", lambda t, ti=ti, nb=nb, m=m: t.matmul(
                    ps_s2[:], lhsT=zf_b[m][:, ti * 128:(ti + 1) * 128],
                    rhs=wkv_nat[m][:, nb * 512:(nb + 1) * 512],
                    start=(m == 0), stop=False),
                    [zf_b[m][:], wkv_nat[m][:]], [ps_s2[:]])
            for m in range(4):
                p.op("t", lambda t, ti=ti, nb=nb, m=m: t.matmul(
                    ps_s2[:], lhsT=zf_b[m][:, ti * 128:(ti + 1) * 128],
                    rhs=worT[m][:, nb * 512:(nb + 1) * 512],
                    start=False, stop=(m == 3)),
                    [zf_b[m][:], worT[m][:]], [ps_s2[:]])
            p.op("v", lambda v, nb=nb: v.tensor_tensor(
                out=ao_t[:, nb * 512:(nb + 1) * 512], in0=ps_s2[:],
                in1=bor_bc[:, nb * 512:(nb + 1) * 512], op=ALU.add),
                [ps_s2[:], bor_bc[:]], [ao_t[:]])
            drain_w12(1)
        xt = xt_buf[ti % 2]
        p.dma("y", xt[:], x_own[ti * 128:(ti + 1) * 128, :])
        layernorm_to(ao_t[:], ao_t, 128, DM, cen_buf[:], cen_buf, si=ti % 2)
        p.op("g", lambda g, xt=xt: g.tensor_tensor(
            out=ao_t[:], in0=cen_buf[:], in1=xt[:], op=ALU.add),
            [cen_buf[:], xt[:]], [ao_t[:]])
        p.dma("y", mid_d[ti * 128:(ti + 1) * 128, :], ao_t[:])
        layernorm_to(ao_t[:], ao_t, 128, DM, ln_b[:], ln_b, si=(ti + 1) % 2,
                     act_apply=True)
        for d in range(16):
            emit_tr(ln_b[:, d * 128:(d + 1) * 128], ln_b,
                    ln1T[d][:, ti * 128:(ti + 1) * 128], ln1T[d])
            if d % 2 == 1:
                drain_w12(1)
    free("ao_t", "zf_3", "zf_2", "zf_1", "zf_0",
         "of_1", "of_0", *(f"vkm_{i}" for i in range(15, -1, -1)),
         "qfb_1", "qfb_0", "kfb_1", "kfb_0",
         "cosQ", "sinQ", "cosF", "sinF", "chTo_1", "chTo_0", "chTf_1",
         "chTf_0",
         *(f"wqT_{i}" for i in range(2)), *(f"wkvcT_{i}" for i in range(2)),
         *(f"wc2lT_{i}" for i in range(2)), *(f"worT_{i}" for i in range(4)),
         *(f"wkvn_{i}" for i in range(4)),
         "qstg_0", "qstg_1", "qw_deqt", "qw_rs1", "qw_wrt", "qw_raw",
         "dq_term", "dq_acc",
         "bor_bc", "bet_bc", "gam_bc", "tcols")

    wstr = [sb(f"wstr_{d}", [128, W12_ROWS * 2], BF16) for d in range(16)]
    h_b = [sb(f"h_{f}", [128, 512], BF16) for f in range(44)]
    dq3stg = [sb(f"dq3stg_{i}", [128, 128], BF16) for i in range(2)]
    dq3sc = sb("dq3sc", [128, 1024], BF16)
    dq3ac = sb("dq3ac", [128, 1024], F32)
    dq3tm = sb("dq3tm", [128, 1024], BF16)

    def stair3(wt_ap, wt_t, rows, nblk, rs1_ap, rs1_t, deqt_ap, deqt_t,
               wrt_ap, wrt_t):
        D = nblk * 64
        p.op("v", lambda v: v.tensor_tensor(
            out=dq3sc[0:rows, 0:D].rearrange("p (a b) -> p a b", b=64),
            in0=wt_ap.rearrange("p (a b) -> p a b", b=64),
            in1=rs1_ap.rearrange("p (a o) -> p a o", o=1)
            .broadcast_to([rows, nblk, 64]),
            op=ALU.mult), [wt_t[:], rs1_t[:]], [dq3sc[:]])
        yield
        accB = ff_f[0:rows, 0:D]
        termB = ff_f[0:rows, 1024:1024 + D]
        p.op("v", lambda v: v.tensor_scalar(out=dq3ac[0:rows, 0:D],
             in0=dq3sc[0:rows, 0:D], scalar1=float(MID[0]),
             scalar2=float(DELTA[0]), op0=ALU.is_gt, op1=ALU.mult),
             [dq3sc[:]], [dq3ac[:]])
        yield
        p.op("v", lambda v: v.tensor_scalar(out=accB,
             in0=dq3sc[0:rows, 0:D], scalar1=float(MID[8]),
             scalar2=float(DELTA[8]), op0=ALU.is_gt, op1=ALU.mult),
             [dq3sc[:]], ["ffB"])
        p.op("v", lambda v: v.tensor_scalar(out=termB,
             in0=dq3sc[0:rows, 0:D], scalar1=float(MID[9]),
             scalar2=float(DELTA[9]), op0=ALU.is_gt, op1=ALU.mult),
             [dq3sc[:]], ["ffT"])
        yield
        for i in range(1, 7):
            p.op("g", lambda g: g.tensor_tensor(out=accB,
                 in0=accB, in1=termB, op=ALU.add), ["ffB", "ffT"], ["ffB"])
            p.op("v", lambda v, j=i: v.tensor_scalar(out=dq3tm[0:rows, 0:D],
                 in0=dq3sc[0:rows, 0:D], scalar1=float(MID[j]),
                 scalar2=float(DELTA[j]), op0=ALU.is_gt, op1=ALU.mult),
                 [dq3sc[:]], [dq3tm[:]])
            p.op("v", lambda v: v.tensor_tensor(out=dq3ac[0:rows, 0:D],
                 in0=dq3ac[0:rows, 0:D], in1=dq3tm[0:rows, 0:D],
                 op=ALU.add), [dq3ac[:], dq3tm[:]], [dq3ac[:]])
            if i < 6:
                p.op("v", lambda v, j=9 + i: v.tensor_scalar(out=termB,
                     in0=dq3sc[0:rows, 0:D], scalar1=float(MID[j]),
                     scalar2=float(DELTA[j]), op0=ALU.is_gt, op1=ALU.mult),
                     [dq3sc[:]], ["ffT"])
            yield
        p.op("v", lambda v: v.tensor_scalar(out=dq3tm[0:rows, 0:D],
             in0=dq3sc[0:rows, 0:D], scalar1=float(MID[7]),
             scalar2=float(DELTA[7]), op0=ALU.is_gt, op1=ALU.mult),
             [dq3sc[:]], [dq3tm[:]])
        p.op("v", lambda v: v.tensor_tensor(out=dq3ac[0:rows, 0:D],
             in0=dq3ac[0:rows, 0:D], in1=dq3tm[0:rows, 0:D],
             op=ALU.add), [dq3ac[:], dq3tm[:]], [dq3ac[:]])
        yield
        p.op("v", lambda v: v.tensor_tensor(out=dq3ac[0:rows, 0:D],
             in0=dq3ac[0:rows, 0:D], in1=accB, op=ALU.add),
             [dq3ac[:], "ffB"], [dq3ac[:]])
        p.op("v", lambda v: v.scalar_tensor_tensor(
            out=wrt_ap.rearrange("p (a b) -> p a b", b=64),
            in0=dq3ac[0:rows, 0:D].rearrange("p (a b) -> p a b", b=64),
            scalar=1.0,
            in1=deqt_ap.rearrange("p (a o) -> p a o", o=1)
            .broadcast_to([rows, nblk, 64]),
            op0=ALU.subtract, op1=ALU.mult), [dq3ac[:], deqt_t[:]], [wrt_t[:]])

    e3t = {}

    def w3_stage(i):
        wrt = e3t[i] = sb(f"e3o_{i}", [128, DFF], BF16)
        raw = sb(f"e3w_{i}", [128, DFF], BF16)
        rs1 = sb(f"e3r_{i}", [128, NB_W3], F32)
        deqt = sb(f"e3q_{i}", [128, NB_W3], BF16)
        p.dma("y", raw[:], w3r[i * 128:(i + 1) * 128, :])
        p.dma("y", rs1[:],
              s1w3_d.rearrange("(a b) -> a b", b=NB_W3)
              [i * 128:(i + 1) * 128, :])
        p.op("v", lambda v, rs1=rs1: v.reciprocal(out=rs1[:], in_=rs1[:]),
             [rs1[:]], [rs1[:]])
        p.dma("y", deqt[:],
              deq3_d.rearrange("(a b) -> a b", b=NB_W3)
              [i * 128:(i + 1) * 128, :])
        for b0 in range(0, NB_W3, 16):
            nb_c = min(16, NB_W3 - b0)
            yield from stair3(
                raw[:, b0 * 64:(b0 + nb_c) * 64], raw, 128, nb_c,
                rs1[:, b0:b0 + nb_c], rs1, deqt[:, b0:b0 + nb_c], deqt,
                wrt[:, b0 * 64:(b0 + nb_c) * 64], wrt)
            yield

    def w3_transposes(i, ag_in):
        wrt = e3t[i]
        for fc in range(DFF // 128):
            stg = dq3stg[fc % 2]
            emit_tr(wrt[:, fc * 128:(fc + 1) * 128], wrt, stg[:], stg,
                    evac="s")
            p.dma("y", ag_in[fc * 128:(fc + 1) * 128, :], stg[:],
                  allow_slow_non_contiguous=True)
            if fc % 4 == 3:
                yield

    def w3_ag_repack(ag_in, ag_out, i):
        p.op("g", lambda g: g.collective_compute(
            "AllGather", ALU.bypass, replica_groups=[list(range(NCORES))],
            ins=[ag_in], outs=[ag_out]), [ag_in], [ag_out],
            forced_inc=True)
        yield
        for k in range(NCORES):
            p.dma("y" if k % 2 == 0 else "s",
                  w3rm[:, k * 256 + i * 128:k * 256 + (i + 1) * 128],
                  ag_out[k, :, :],
                  wkey=f"w3rm#{k}", rkeys=[ag_out],
                  allow_slow_non_contiguous=True)
        yield

    def w3_dequant_queue():
        yield from w3_stage(0)
        yield from w3_transposes(0, agw3_inA)
        free("e3q_0", "e3r_0", "e3w_0", "e3o_0")
        yield
        yield from w3_stage(1)
        yield from w3_ag_repack(agw3_inA, agw3_outA, 0)
        yield from w3_transposes(1, agw3_inB)
        free("e3q_1", "e3r_1", "e3w_1", "e3o_1")
        yield
        yield from w3_ag_repack(agw3_inB, agw3_outB, 1)

    w3q = w3_dequant_queue()
    w3str = [sb(f"w3str_{i}", [128, 1024], BF16) for i in range(2)]
    ff4 = [ff_f, None, None, None]

    def drain_q(n):
        for _ in range(n):
            try:
                next(w3q)
            except StopIteration:
                return

    def ab_phase(ag_out, half, is_b, ps_abs, drain=0):
        for r in range(NCORES):
            bp = r % 2
            if drain:
                drain_q(drain)
            for d in range(16):
                p.dma("y" if d % 2 == 0 else "s",
                      wstr[d][:, bp * W12_ROWS:(bp + 1) * W12_ROWS],
                      ag_out[r, d * 128:(d + 1) * 128, :])
            chunks = []
            g = W12_ROWS * r
            rem = W12_ROWS
            while rem:
                c = min(128 - g % 128, rem)
                chunks.append((g - W12_ROWS * r, c))
                g += c
                rem -= c
            for ci, (fo, fw) in enumerate(chunks):
                ps_ab = ps_abs[ci % 2]
                if drain:
                    drain_q(1)
                for d in range(16):
                    p.op("t", lambda t, d=d, bp=bp, fo=fo, fw=fw, half=half,
                         ps_ab=ps_ab: t.matmul(
                        ps_ab[0:fw, :],
                        lhsT=wstr[d][:, bp * W12_ROWS + fo:
                                     bp * W12_ROWS + fo + fw],
                        rhs=ln1T[d][:, half * 512:(half + 1) * 512],
                        start=(d == 0), stop=(d == 15)),
                        [wstr[d][:], ln1T[d][:]], [ps_ab[:]])
                gpos = r * W12_ROWS + fo
                hi, ho = gpos // 128, gpos % 128
                assert ho + fw <= 128, (r, fo, fw)
                if not is_b:
                    p.op("s", lambda s, hi=hi, ho=ho, fw=fw, ps_ab=ps_ab:
                         s.activation(out=h_b[hi][ho:ho + fw, :],
                                      in_=ps_ab[0:fw, :], func=AF.Silu),
                         [ps_ab[:]], [h_b[hi][:]])
                else:
                    p.op("v", lambda v, hi=hi, ho=ho, fw=fw, ps_ab=ps_ab:
                         v.tensor_tensor(out=h_b[hi][ho:ho + fw, :],
                                         in0=h_b[hi][ho:ho + fw, :],
                                         in1=ps_ab[0:fw, :], op=ALU.mult),
                         [h_b[hi][:], ps_ab[:]], [h_b[hi][:]])

    def w3_phase(half, ps_fs):
        for dmh in range(2):
            for fc in range(44):
                buf = w3str[fc % 2]
                p.dma("y" if fc % 2 == 0 else "s", buf[:],
                      w3rm[fc * 128:(fc + 1) * 128,
                           dmh * 1024:(dmh + 1) * 1024],
                      wkey=f"w3s{fc % 2}",
                      rkeys=[f"w3rm#{j}" for j in range(NCORES)])
                for tl in range(4):
                    for nb in range(2):
                        p.op("t", lambda t, fc=fc, nb=nb, tl=tl, buf=buf:
                             t.matmul(
                            ps_fs[tl][:, nb * 512:(nb + 1) * 512],
                            lhsT=h_b[fc][:, tl * 128:(tl + 1) * 128],
                            rhs=buf[:, nb * 512:(nb + 1) * 512],
                            start=(fc == 0), stop=(fc == 43)),
                            [h_b[fc][:], f"w3s{fc % 2}"],
                            [ps_fs[tl][:]])
            for tl in range(4):
                p.op("s", lambda s, tl=tl, dmh=dmh: s.copy(
                    out=ff4[tl][:, dmh * 1024:(dmh + 1) * 1024],
                    in_=ps_fs[tl][:]), [ps_fs[tl][:]], [ff4[tl][:]])
        for tl in range(4):
            tc = half * 4 + tl
            xt = xt_buf[tc % 2]
            p.dma("y", xt[:], mid_d[tc * 128:(tc + 1) * 128, :])
            layernorm_to(ff4[tl][:], ff4[tl], 128, DM, cen_buf[:], cen_buf,
                         si=tc % 2)
            p.op("g", lambda g, xt=xt, tl=tl: g.tensor_tensor(
                out=ff4[tl][:], in0=cen_buf[:], in1=xt[:], op=ALU.add),
                [cen_buf[:], xt[:]], [ff4[tl][:]])
            p.dma("y", out[tc * 128:(tc + 1) * 128, :], ff4[tl][:])

    drain_w12(10 ** 6)
    free_ps("ps_sc", "ps_s2", "ps_s1")
    for half in range(2):
        ps_abs = [ps(f"ps_ab{half}_0", [128, 512], F32),
                  ps(f"ps_ab{half}_1", [128, 512], F32)]
        ab_phase(agw1_out, half, is_b=False, ps_abs=ps_abs,
                 drain=3 if half == 0 else 0)
        ab_phase(agw2_out, half, is_b=True, ps_abs=ps_abs,
                 drain=3 if half == 0 else 0)
        if half == 0:
            drain_q(10 ** 6)
            for j in range(1, 4):
                ff4[j] = sb(f"ff4_{j}", [128, DM], F32)
            free_ps(f"ps_ab{half}_1", f"ps_ab{half}_0", "ps_tr2", "ps_tr")
        else:
            free_ps(f"ps_ab{half}_1", f"ps_ab{half}_0")
        ps_fs = [ps(f"ps_f{half}_{j}", [128, 1024], F32) for j in range(4)]
        w3_phase(half, ps_fs)
        free_ps(*(f"ps_f{half}_{j}" for j in range(3, -1, -1)))

    sems = ExitStack()
    block = nc.Block()
    block.__enter__()
    p.finalize(block, sems)
    block.__exit__(None, None, None)
    for n in list(cms):
        if n.startswith("ps_"):
            cms.pop(n).__exit__(None, None, None)
    for n in reversed(stack):
        cms.pop(n).__exit__(None, None, None)
    for cm in list(cms.values()):
        cm.__exit__(None, None, None)
    sems.close()
    return nc


def kernel(L0_attn_x, L0_attn_W_to_kv, L0_attn_W_compress, L0_attn_ln_gamma,
           L0_attn_ln_beta, L0_attn_W_q_comp, L0_attn_W_kv_comp,
           L0_attn_W_comp2lat, L0_attn_W_out_res, L0_attn_b_out_res,
           L0_ffn_weight1, L0_ffn_weight2, L0_ffn_weight3):
    x = np.ascontiguousarray(np.asarray(L0_attn_x, dtype=np.float32))
    wkv_h = np.ascontiguousarray(np.asarray(L0_attn_W_to_kv, dtype=np.float32))
    wcmp_h = np.ascontiguousarray(np.asarray(L0_attn_W_compress, dtype=np.float32))
    gam_h = np.asarray(L0_attn_ln_gamma, dtype=np.float32).reshape(1, DC)
    bet_h = np.asarray(L0_attn_ln_beta, dtype=np.float32).reshape(1, DC)
    wq_h = np.ascontiguousarray(np.asarray(L0_attn_W_q_comp, dtype=np.float32))
    wkvc_h = np.ascontiguousarray(np.asarray(L0_attn_W_kv_comp, dtype=np.float32))
    wc2l_h = np.ascontiguousarray(np.asarray(L0_attn_W_comp2lat, dtype=np.float32))
    wor_h = np.ascontiguousarray(np.asarray(L0_attn_W_out_res, dtype=np.float32))
    bor_h = np.asarray(L0_attn_b_out_res, dtype=np.float32).reshape(1, DM)
    w1 = np.asarray(L0_ffn_weight1)
    w2 = np.asarray(L0_ffn_weight2)
    w3 = np.asarray(L0_ffn_weight3)

    nc = build_program()
    in_maps = []
    for c in range(NCORES):
        b, h = c // 2, c % 2
        in_maps.append({
            "x_full": x[b],
            "x_own": np.ascontiguousarray(x[b, h * OWN:(h + 1) * OWN]),
            "rowbase": np.array([[h * OWN]], dtype=np.float32),
            "oh": (np.arange(NCORES)[None, :] // 2 == b)
            .astype(np.float32),
            "wkv": wkv_h, "wcmp": wcmp_h, "gam": gam_h, "bet": bet_h,
            "wq": wq_h, "wkvc": wkvc_h, "wc2l": wc2l_h, "wor": wor_h,
            "bor": bor_h,
            "w1s": np.ascontiguousarray(w1[c * W12_ROWS:(c + 1) * W12_ROWS]),
            "w2s": np.ascontiguousarray(w2[c * W12_ROWS:(c + 1) * W12_ROWS]),
            "w3r": np.ascontiguousarray(w3[c * W3_ROWS:(c + 1) * W3_ROWS]),
        })
    res = run_bass_kernel_spmd(nc, in_maps, core_ids=list(range(NCORES)))
    outp = np.empty((B, T, DM), dtype=np.float32)
    for c in range(NCORES):
        b, h = c // 2, c % 2
        outp[b, h * OWN:(h + 1) * OWN] = res.results[c]["out"]
    return outp


# version 36
# speedup vs baseline: 1.0131x; 1.0131x over previous
import math
from contextlib import ExitStack

import numpy as np
import ml_dtypes
import concourse.bass as bass
import concourse.mybir as mybir
from concourse.bass_utils import run_bass_kernel_spmd

F32 = mybir.dt.float32
BF16 = mybir.dt.bfloat16
I32 = mybir.dt.int32
AF = mybir.ActivationFunctionType
ALU = mybir.AluOpType
AXX = mybir.AxisListType.X

B, T, DM = 4, 2048, 2048
DL, DC, DFF = 512, 256, 5632
NCORES = 8
OWN = T // 2
LN_EPS = 1e-5
SCALE = 1.0 / 16.0
NEG = -1.0e30

NF4 = np.array([-1.0, -0.6962, -0.5251, -0.3949, -0.2844, -0.1848, -0.0911, 0.0,
                0.0796, 0.1609, 0.2461, 0.3379, 0.4407, 0.5626, 0.723, 1.0],
               dtype=ml_dtypes.bfloat16).astype(np.float32)
MID = ((NF4[:-1] + NF4[1:]) / 2.0).astype(np.float32)
DELTA = (NF4[1:] - NF4[:-1]).astype(np.float32)

TWO_PI = 2.0 * math.pi


def _split12(x):
    ui = np.float32(x).view(np.uint32) & np.uint32(0xFFFFF000)
    return float(ui.view(np.float32))


CW1 = _split12(TWO_PI)
CW2 = _split12(TWO_PI - CW1)
CW3 = float(np.float32(TWO_PI - CW1 - CW2))
INV2PI = float(np.float32(1.0 / TWO_PI))
PI_F = float(np.float32(math.pi))
PI_LO = float(np.nextafter(np.float32(math.pi), np.float32(0.0)))
PH1 = _split12(math.pi / 2)
PH2 = float(np.float32(math.pi / 2 - PH1))
LNTH = math.log(10000.0)

W12_ROWS = DFF // NCORES
W3_ROWS = DM // NCORES
NB_W12 = DM // 64
NB_W3 = DFF // 64
NSB_W12 = W12_ROWS * NB_W12 // 256
NSCAL_W3 = W3_ROWS * NB_W3
AG1_SHARD = NSCAL_W3 + 16
W12_RC = [128] * 5 + [64]


class Prog:
    def __init__(self, nc):
        self.nc = nc
        self.ops = []
        self.version = {}
        self.readers = {}
        self._floor = None

    @staticmethod
    def key(ap):
        if isinstance(ap, str):
            return ap
        t = getattr(ap, "tensor", ap)
        return t.name

    def _record(self, eng, kind, fn, reads, writes, dma_key=None, forced_inc=False):
        idx = len(self.ops)
        edges = set()
        if self._floor is not None:
            edges.add(self._floor)
        for ap in reads:
            w = self.version.get(self.key(ap))
            if w is not None:
                edges.add(w)
        for ap in writes:
            k = self.key(ap)
            w = self.version.get(k)
            if w is not None:
                edges.add(w)
            for r in self.readers.get(k, ()):
                edges.add(r)
        final_edges = set()
        for eidx in edges:
            if eidx == idx:
                continue
            if eng == "t" and self.ops[eidx]["eng"] == "t":
                continue
            final_edges.add(eidx)
        for ap in reads:
            self.readers.setdefault(self.key(ap), []).append(idx)
        for ap in writes:
            k = self.key(ap)
            self.version[k] = idx
            self.readers[k] = []
        self.ops.append(dict(eng=eng, kind=kind, fn=fn, edges=sorted(final_edges),
                             dma_key=dma_key, forced_inc=forced_inc))
        return idx

    def op(self, eng, fn, reads, writes, forced_inc=False):
        return self._record(eng, "op", fn, reads, writes, forced_inc=forced_inc)

    def dma(self, eng, dst, src, wkey=None, rkeys=None, **kw):
        def fn(e):
            return e.dma_start(out=dst, in_=src, **kw)
        wk = wkey if wkey is not None else self.key(dst)
        reads = rkeys if rkeys is not None else [src]
        return self._record(eng, "dma", fn, reads, [wk], dma_key=wk)

    def barrier(self):
        last = {}
        for i, o in enumerate(self.ops):
            if o["kind"] == "dma":
                last[("d", o["dma_key"])] = i
            else:
                last[o["eng"]] = i
        idx = len(self.ops)
        edges = sorted(v for v in last.values() if v != idx)
        self.ops.append(dict(eng="v", kind="op", fn=lambda e: e.nop(),
                             edges=edges, dma_key=None, forced_inc=False,
                             barrier=True))
        self._floor = idx
        self.version = {}
        self.readers = {}

    def finalize(self, block, sems_ctx):
        nc = self.nc
        needs_inc = [False] * len(self.ops)
        for o in self.ops:
            for e in o["edges"]:
                needs_inc[e] = True
        for i, o in enumerate(self.ops):
            if o["kind"] == "dma" or o["forced_inc"]:
                needs_inc[i] = True
        eng_sem = {}
        for eng in "vstg":
            eng_sem[eng] = sems_ctx.enter_context(nc.semaphore(f"se_{eng}"))
        sem_pool = []
        n_sems = 0
        key2sem = {}
        sem_count = {}
        eng_count = {e: 0 for e in "vstgy"}
        inc_info = [None] * len(self.ops)
        for i, o in enumerate(self.ops):
            if o.get("barrier"):
                for k in list(key2sem):
                    sem_pool.append(key2sem.pop(k))
            if o["kind"] == "dma":
                k = o["dma_key"]
                if k not in key2sem:
                    if sem_pool:
                        key2sem[k] = sem_pool.pop()
                    else:
                        h = sems_ctx.enter_context(nc.semaphore(f"sd_{n_sems}"))
                        n_sems += 1
                        sem_count[h.name] = 0
                        key2sem[k] = h
                h = key2sem[k]
                sem_count[h.name] += 16
                inc_info[i] = (h, sem_count[h.name], 16)
            elif needs_inc[i]:
                assert o["eng"] != "y"
                eng_count[o["eng"]] += 1
                inc_info[i] = (eng_sem[o["eng"]], eng_count[o["eng"]], 1)
        waited = {e: {} for e in "vstgy"}
        all_sems = {}
        for info in inc_info:
            if info is not None and info[2] == 16:
                all_sems[info[0].name] = (info[0], max(
                    all_sems.get(info[0].name, (None, 0))[1], info[1]))
        dma_final = [(v, h) for h, v in all_sems.values()]

        def emit_engine(eng, engobj):
            for i, o in enumerate(self.ops):
                if o["eng"] != eng:
                    continue
                for eidx in o["edges"]:
                    info = inc_info[eidx]
                    assert info is not None, (i, eidx)
                    sem, val, _ = info
                    if waited[eng].get(sem.name, -1) < val:
                        engobj.wait_ge(sem, val)
                        waited[eng][sem.name] = val
                ins = o["fn"](engobj)
                if inc_info[i] is not None:
                    sem, val, step = inc_info[i]
                    ins.then_inc(sem, step)

        @block.sync
        def _(e):
            emit_engine("y", e)
            for v, sem in dma_final:
                e.wait_ge(sem, v)

        @block.vector
        def _(e):
            emit_engine("v", e)

        @block.scalar
        def _(e):
            emit_engine("s", e)

        @block.tensor
        def _(e):
            emit_engine("t", e)

        @block.gpsimd
        def _(e):
            emit_engine("g", e)


def build_program():
    nc = bass.Bass("TRN2", target_bir_lowering=False, debug=False)
    x_full = nc.dram_tensor("x_full", [T, DM], F32, kind="ExternalInput").ap()
    x_own = nc.dram_tensor("x_own", [OWN, DM], F32, kind="ExternalInput").ap()
    rowbase = nc.dram_tensor("rowbase", [1, 1], F32, kind="ExternalInput").ap()
    oh_d = nc.dram_tensor("oh", [1, NCORES], F32, kind="ExternalInput").ap()
    wkv = nc.dram_tensor("wkv", [DL, DM], F32, kind="ExternalInput").ap()
    wcmp = nc.dram_tensor("wcmp", [DC, DL], F32, kind="ExternalInput").ap()
    gam = nc.dram_tensor("gam", [1, DC], F32, kind="ExternalInput").ap()
    bet = nc.dram_tensor("bet", [1, DC], F32, kind="ExternalInput").ap()
    wq = nc.dram_tensor("wq", [DC, DC], F32, kind="ExternalInput").ap()
    wkvc = nc.dram_tensor("wkvc", [2 * DC, DC], F32, kind="ExternalInput").ap()
    wc2l = nc.dram_tensor("wc2l", [DL, DC], F32, kind="ExternalInput").ap()
    wor = nc.dram_tensor("wor", [DM, DL], F32, kind="ExternalInput").ap()
    bor = nc.dram_tensor("bor", [1, DM], F32, kind="ExternalInput").ap()
    w1s = nc.dram_tensor("w1s", [W12_ROWS, DM], BF16, kind="ExternalInput").ap()
    w2s = nc.dram_tensor("w2s", [W12_ROWS, DM], BF16, kind="ExternalInput").ap()
    w3r = nc.dram_tensor("w3r", [W3_ROWS, DFF], BF16, kind="ExternalInput").ap()
    out = nc.dram_tensor("out", [OWN, DM], F32, kind="ExternalOutput").ap()

    s1w1_d = nc.dram_tensor("s1w1_d", [W12_ROWS * NB_W12], F32, kind="Internal").ap()
    s1w2_d = nc.dram_tensor("s1w2_d", [W12_ROWS * NB_W12], F32, kind="Internal").ap()
    s1w3_d = nc.dram_tensor("s1w3_d", [NSCAL_W3], F32, kind="Internal").ap()
    deq1_d = nc.dram_tensor("deq1_d", [W12_ROWS * NB_W12], BF16, kind="Internal").ap()
    deq2_d = nc.dram_tensor("deq2_d", [W12_ROWS * NB_W12], BF16, kind="Internal").ap()
    deq3_d = nc.dram_tensor("deq3_d", [NSCAL_W3], BF16, kind="Internal").ap()
    ag1_in = nc.dram_tensor("ag1_in", [AG1_SHARD], F32, kind="Internal").ap()
    ag1_out = nc.dram_tensor("ag1_out", [NCORES * AG1_SHARD], F32,
                             kind="Internal").ap()
    agw1_in = nc.dram_tensor("agw1_in", [DM, W12_ROWS], BF16, kind="Internal").ap()
    agw2_in = nc.dram_tensor("agw2_in", [DM, W12_ROWS], BF16, kind="Internal").ap()
    agw3_inA = nc.dram_tensor("agw3_inA", [DFF, 128], BF16,
                              kind="Internal").ap()
    agw3_inB = nc.dram_tensor("agw3_inB", [DFF, 128], BF16,
                              kind="Internal").ap()
    agw1_out = nc.dram_tensor("agw1_out", [NCORES, DM, W12_ROWS], BF16,
                              kind="Internal", addr_space="Shared").ap()
    agw2_out = nc.dram_tensor("agw2_out", [NCORES, DM, W12_ROWS], BF16,
                              kind="Internal", addr_space="Shared").ap()
    agw3_outA = nc.dram_tensor("agw3_outA", [NCORES, DFF, 128], BF16,
                               kind="Internal", addr_space="Shared").ap()
    agw3_outB = nc.dram_tensor("agw3_outB", [NCORES, DFF, 128], BF16,
                               kind="Internal", addr_space="Shared").ap()
    w3rm = nc.dram_tensor("w3rm", [DFF, DM], BF16, kind="Internal").ap()
    chx_in = nc.dram_tensor("chx_in", [2, 128, OWN], BF16,
                            kind="Internal").ap()
    chx_out = nc.dram_tensor("chx_out", [NCORES, 2, 128, OWN], BF16,
                             kind="Internal", addr_space="Shared").ap()
    mid_d = nc.dram_tensor("mid_d", [OWN, DM], F32, kind="Internal").ap()
    mb_d = nc.dram_tensor("mb_d", [1, 4], F32, kind="Internal").ap()

    p = Prog(nc)
    stack = []
    cms = {}

    def sb(name, shape, dtype):
        cm = nc.sbuf_tensor(name, shape, dtype)
        t = cm.__enter__()
        cms[name] = cm
        stack.append(name)
        return t

    def ps(name, shape, dtype=F32):
        cm = nc.psum_tensor(name, shape, dtype)
        t = cm.__enter__()
        cms[name] = cm
        return t

    def free(*names):
        ns = set(names)
        assert ns == set(stack[-len(ns):]), (
            f"non-suffix free: {sorted(ns)} vs {stack[-len(ns):]}")
        for n in reversed(stack[-len(ns):]):
            cms.pop(n).__exit__(None, None, None)
        del stack[-len(ns):]
        p.barrier()

    def free_ps(*names):
        for n in names:
            cms.pop(n).__exit__(None, None, None)
        p.barrier()

    ident_b = sb("ident_b", [128, 128], BF16)
    ones = sb("ones", [128, 1], F32)
    iota_p = sb("iota_p", [128, 1], F32)
    rb_bc = sb("rb_bc", [128, 1], F32)
    mscalar = sb("mscalar", [1, 4], F32)
    meanbc = sb("meanbc", [128, 3], F32)
    ln_mu = [sb(f"ln_mu{i}", [128, 1], F32) for i in range(2)]
    ln_ss = [sb(f"ln_ss{i}", [128, 1], F32) for i in range(2)]
    ln_rs = [sb(f"ln_rs{i}", [128, 1], F32) for i in range(2)]
    ln_nm = [sb(f"ln_nm{i}", [128, 1], F32) for i in range(2)]
    rowlim = sb("rowlim", [128, 1], F32)
    rowmax = sb("rowmax", [128, 1], F32)
    zsum = sb("zsum", [128, 1], F32)
    cen_buf = sb("cen_buf", [128, DM], F32)
    ff_f = sb("ff_f", [128, DM], F32)
    ln_b = sb("ln_b", [128, DM], BF16)
    xt_buf = [sb(f"xt_{i}", [128, DM], F32) for i in range(2)]
    ln1T = [sb(f"ln1T_{d}", [128, OWN], BF16) for d in range(16)]

    ps_tr = ps("ps_tr", [128, 128], BF16)
    ps_tr2 = ps("ps_tr2", [128, 128], BF16)
    ps_s1 = ps("ps_s1", [128, 512], F32)
    ps_s2 = ps("ps_s2", [128, 512], F32)
    ps_sc = ps("ps_sc", [128, 2048], F32)


    tr_ct = [0]

    def emit_tr(src_ap, src_t, dst_ap, dst_t, r=128, evac=None):
        pst = ps_tr if tr_ct[0] % 2 == 0 else ps_tr2
        eng = evac if evac is not None else ("v" if tr_ct[0] % 4 == 3 else "s")
        tr_ct[0] += 1
        p.op("t", lambda t, pst=pst, r=r: t.transpose(
            pst[:, 0:r], src_ap, ident_b[0:r, 0:r]),
            [src_t[:], ident_b[:]], [pst[:]])
        if eng == "v":
            p.op("v", lambda v, pst=pst, r=r: v.tensor_copy(
                out=dst_ap, in_=pst[:, 0:r]), [pst[:]], [dst_t[:]])
        else:
            p.op("s", lambda s, pst=pst, r=r: s.copy(
                out=dst_ap, in_=pst[:, 0:r]), [pst[:]], [dst_t[:]])

    ic = sb("ic", [128, 128], F32)
    ir = sb("ir", [128, 128], F32)
    p.op("g", lambda g: g.iota(ic[:], pattern=[[1, 128]], base=0,
         channel_multiplier=0, allow_small_or_imprecise_dtypes=True), [], [ic[:]])
    p.op("g", lambda g: g.iota(ir[:], pattern=[[0, 128]], base=0,
         channel_multiplier=1, allow_small_or_imprecise_dtypes=True), [], [ir[:]])
    p.op("v", lambda v: v.tensor_tensor(out=ident_b[:], in0=ic[:], in1=ir[:],
         op=ALU.is_equal), [ic[:], ir[:]], [ident_b[:]])
    p.op("v", lambda v: v.memset(ones[:], 1.0), [], [ones[:]])
    p.op("v", lambda v: v.tensor_copy(out=iota_p[:], in_=ir[:, 0:1]),
         [ir[:]], [iota_p[:]])
    p.dma("y", rb_bc[:], rowbase.broadcast_to([128, 1]))
    free("ic", "ir")

    tcols = sb("tcols", [128, T], F32)
    gam_bc = sb("gam_bc", [128, DC], BF16)
    bet_bc = sb("bet_bc", [128, DC], BF16)
    bor_bc = sb("bor_bc", [128, DM], F32)
    p.op("g", lambda g: g.iota(tcols[:], pattern=[[1, T]], base=0,
         channel_multiplier=0, allow_small_or_imprecise_dtypes=True),
         [], [tcols[:]])
    p.dma("y", ff_f[:, 0:DC], gam.broadcast_to([128, DC]))
    p.op("v", lambda v: v.tensor_copy(out=gam_bc[:], in_=ff_f[:, 0:DC]),
         [ff_f[:]], [gam_bc[:]])
    p.dma("y", ff_f[:, DC:2 * DC], bet.broadcast_to([128, DC]))
    p.op("v", lambda v: v.tensor_copy(out=bet_bc[:], in_=ff_f[:, DC:2 * DC]),
         [ff_f[:]], [bet_bc[:]])
    p.dma("y", bor_bc[:], bor.broadcast_to([128, DM]))


    dq_acc = sb("dq_acc", [128, 1024], F32)
    dq_term = sb("dq_term", [128, 1024], BF16)
    dq_stage = sb("dq_stage", [128, W12_ROWS], BF16)
    rowsum = sb("rowsum", [128, 1], F32)
    sq_cen = sb("sq_cen", [128, 256], F32)
    sq_sabs = sb("sq_sabs", [128, 1], F32)
    sq_qf = sb("sq_qf", [128, 1], F32)
    sq_rqf = sb("sq_rqf", [128, 1], F32)
    sq_a = sb("sq_a", [128, 256], F32)
    sq_i = sb("sq_i", [128, 256], I32)
    sq_b = sb("sq_b", [128, 256], F32)
    sq_bb = sb("sq_bb", [128, 256], BF16)
    sq_b2 = sb("sq_b2", [128, 256], BF16)

    def pass1(dram, shape_rows, ncols, nblk, s1_dst, sum_col, tag):
        off = 0
        i = 0
        while off < shape_rows:
            r = min(128, shape_rows - off)
            wt = sb(f"{tag}p1_{i}", [r, ncols], BF16)
            p.dma("y", wt[:], dram[off:off + r, :])
            p.op("v", lambda v, wt=wt, r=r, nblk=nblk: v.tensor_reduce(
                out=dq_acc[0:r, 0:nblk],
                in_=wt[:].rearrange("p (a b) -> p a b", b=64),
                axis=AXX, op=ALU.max, apply_absolute_value=True),
                [wt[:]], [dq_acc[:]])
            p.op("v", lambda v, r=r, nblk=nblk: v.tensor_reduce(
                out=rowsum[0:r, :], in_=dq_acc[0:r, 0:nblk], axis=AXX,
                op=ALU.add), [dq_acc[:]], [rowsum[:]])
            p.op("t", lambda t, r=r, f=(i == 0),
                 l=(off + r >= shape_rows): t.matmul(
                ps_s1[0:1, 0:1], lhsT=ones[0:r, :], rhs=rowsum[0:r, :],
                start=f, stop=l), [rowsum[:], ones[:]], [ps_s1[:]])
            p.dma("s", s1_dst.rearrange("(a b) -> a b", b=nblk)[off:off + r, :],
                  dq_acc[0:r, 0:nblk])
            off += r
            i += 1
        p.op("s", lambda s, c=sum_col: s.activation(
            out=mscalar[:, c:c + 1], in_=ps_s1[0:1, 0:1], func=AF.Copy),
            [ps_s1[:]], [mscalar[:]])
        free(*(f"{tag}p1_{j}" for j in range(i)))

    pass1(w1s, W12_ROWS, DM, NB_W12, s1w1_d, 0, "w1")
    pass1(w2s, W12_ROWS, DM, NB_W12, s1w2_d, 1, "w2")
    pass1(w3r, W3_ROWS, DFF, NB_W3, s1w3_d, 2, "w3")
    padz = sb("padz", [1, AG1_SHARD - NSCAL_W3], F32)
    p.op("v", lambda v: v.memset(padz[:], 0.0), [], [padz[:]])
    p.dma("s", ag1_in[0:NSCAL_W3].rearrange("(a b) -> a b", b=176),
          s1w3_d.rearrange("(a b) -> a b", b=176))
    p.dma("s", ag1_in[NSCAL_W3:AG1_SHARD].rearrange("(a b) -> a b", a=1), padz[:])
    p.dma("s", ag1_in[NSCAL_W3:NSCAL_W3 + 2].rearrange("(a b) -> a b", a=1),
          mscalar[0:1, 0:2])
    free("padz")
    p.op("g", lambda g: g.collective_compute(
        "AllGather", ALU.bypass, replica_groups=[list(range(NCORES))],
        ins=[ag1_in], outs=[ag1_out]), [ag1_in], [ag1_out], forced_inc=True)

    sumt = sb("sumt", [1, NCORES], F32)
    w3sc = sb("w3sc", [128, 176], F32)
    ag1v = ag1_out.rearrange("(k s) -> k s", s=AG1_SHARD)
    for j, total in ((0, DFF * NB_W12), (1, DFF * NB_W12)):
        p.dma("s", sumt[0:1, 0:NCORES], ag1v[:, NSCAL_W3 + j:NSCAL_W3 + j + 1],
              allow_slow_non_contiguous=True)
        p.op("v", lambda v: v.tensor_reduce(out=mscalar[0:1, 3:4], in_=sumt[:],
             axis=AXX, op=ALU.add), [sumt[:]], [mscalar[:]])
        p.op("s", lambda s, j=j, total=total: s.activation(
            out=mscalar[0:1, j:j + 1], in_=mscalar[0:1, 3:4], func=AF.Copy,
            scale=1.0 / total), [mscalar[:]], [mscalar[:]])
    for k in range(NCORES):
        p.dma("s", w3sc[:],
              ag1_out[k * AG1_SHARD:k * AG1_SHARD + NSCAL_W3]
              .rearrange("(a b) -> a b", b=176))
        p.op("v", lambda v: v.tensor_reduce(out=rowsum[:], in_=w3sc[:],
             axis=AXX, op=ALU.add), [w3sc[:]], [rowsum[:]])
        p.op("t", lambda t, f=(k == 0), l=(k == NCORES - 1): t.matmul(
            ps_s1[0:1, 0:1], lhsT=ones[:], rhs=rowsum[:], start=f, stop=l),
            [rowsum[:], ones[:]], [ps_s1[:]])
    p.op("s", lambda s: s.activation(out=mscalar[0:1, 2:3],
         in_=ps_s1[0:1, 0:1], func=AF.Copy, scale=1.0 / (DM * NB_W3)),
         [ps_s1[:]], [mscalar[:]])
    msc_b = sb("msc_b", [1, 4], BF16)
    p.op("s", lambda s_: s_.copy(out=msc_b[:], in_=mscalar[:]),
         [mscalar[:]], [msc_b[:]])
    p.op("s", lambda s_: s_.copy(out=mscalar[:], in_=msc_b[:]),
         [msc_b[:]], [mscalar[:]])
    p.dma("s", mb_d, mscalar[0:1, 0:4])
    p.dma("s", meanbc[:, 0:3], mb_d[0:1, 0:3].broadcast_to([128, 3]))
    free("msc_b")
    free("sumt", "w3sc")

    def scaler_quant(src_view, n_sb, mean_col, dst_view):
        p.dma("s", sq_cen[0:n_sb, :], src_view)
        p.op("v", lambda v: v.tensor_scalar(out=sq_bb[0:n_sb, :],
             in0=sq_cen[0:n_sb, :],
             scalar1=meanbc[0:n_sb, mean_col:mean_col + 1],
             scalar2=None, op0=ALU.subtract), [sq_cen[:], meanbc[:]],
             [sq_bb[:]])
        p.op("v", lambda v: v.tensor_copy(out=sq_cen[0:n_sb, :],
             in_=sq_bb[0:n_sb, :]), [sq_bb[:]], [sq_cen[:]])
        p.op("v", lambda v: v.tensor_reduce(out=sq_sabs[0:n_sb, :],
             in_=sq_cen[0:n_sb, :], axis=AXX, op=ALU.max,
             apply_absolute_value=True), [sq_cen[:]], [sq_sabs[:]])
        p.op("v", lambda v: v.reciprocal(out=sq_qf[0:n_sb, :],
             in_=sq_sabs[0:n_sb, :]), [sq_sabs[:]], [sq_qf[:]])
        p.op("v", lambda v: v.tensor_scalar(out=sq_bb[0:n_sb, 0:1],
             in0=sq_qf[0:n_sb, :], scalar1=128.0, scalar2=None, op0=ALU.mult),
             [sq_qf[:]], [sq_bb[:]])
        p.op("v", lambda v: v.tensor_copy(out=sq_qf[0:n_sb, :],
             in_=sq_bb[0:n_sb, 0:1]), [sq_bb[:]], [sq_qf[:]])
        p.op("v", lambda v: v.reciprocal(out=sq_rqf[0:n_sb, :],
             in_=sq_qf[0:n_sb, :]), [sq_qf[:]], [sq_rqf[:]])
        p.op("v", lambda v: v.tensor_scalar(out=sq_bb[0:n_sb, :],
             in0=sq_cen[0:n_sb, :], scalar1=sq_qf[0:n_sb, 0:1], scalar2=None,
             op0=ALU.mult), [sq_cen[:], sq_qf[:]], [sq_bb[:]])
        p.op("v", lambda v: v.tensor_copy(out=sq_a[0:n_sb, :],
             in_=sq_bb[0:n_sb, :]), [sq_bb[:]], [sq_a[:]])
        p.op("v", lambda v: v.tensor_copy(out=sq_i[0:n_sb, :],
             in_=sq_a[0:n_sb, :]), [sq_a[:]], [sq_i[:]])
        p.op("v", lambda v: v.tensor_copy(out=sq_cen[0:n_sb, :],
             in_=sq_i[0:n_sb, :]), [sq_i[:]], [sq_cen[:]])
        p.op("v", lambda v: v.tensor_tensor(out=sq_a[0:n_sb, :],
             in0=sq_a[0:n_sb, :], in1=sq_cen[0:n_sb, :], op=ALU.subtract),
             [sq_a[:], sq_cen[:]], [sq_a[:]])
        p.op("v", lambda v: v.tensor_scalar(out=sq_b[0:n_sb, :],
             in0=sq_a[0:n_sb, :], scalar1=0.5, scalar2=None, op0=ALU.is_gt),
             [sq_a[:]], [sq_b[:]])
        p.op("v", lambda v: v.tensor_tensor(out=sq_cen[0:n_sb, :],
             in0=sq_cen[0:n_sb, :], in1=sq_b[0:n_sb, :], op=ALU.add),
             [sq_cen[:], sq_b[:]], [sq_cen[:]])
        p.op("v", lambda v: v.tensor_scalar(out=sq_b[0:n_sb, :],
             in0=sq_a[0:n_sb, :], scalar1=-0.5, scalar2=None, op0=ALU.is_lt),
             [sq_a[:]], [sq_b[:]])
        p.op("v", lambda v: v.tensor_tensor(out=sq_cen[0:n_sb, :],
             in0=sq_cen[0:n_sb, :], in1=sq_b[0:n_sb, :], op=ALU.subtract),
             [sq_cen[:], sq_b[:]], [sq_cen[:]])
        p.op("v", lambda v: v.tensor_scalar(out=sq_cen[0:n_sb, :],
             in0=sq_cen[0:n_sb, :], scalar1=127.0, scalar2=-128.0,
             op0=ALU.min, op1=ALU.max), [sq_cen[:]], [sq_cen[:]])
        p.op("v", lambda v: v.tensor_scalar(out=sq_bb[0:n_sb, :],
             in0=sq_cen[0:n_sb, :], scalar1=sq_rqf[0:n_sb, 0:1], scalar2=None,
             op0=ALU.mult), [sq_cen[:], sq_rqf[:]], [sq_bb[:]])
        p.op("v", lambda v: v.tensor_scalar(out=sq_b2[0:n_sb, :],
             in0=sq_bb[0:n_sb, :],
             scalar1=meanbc[0:n_sb, mean_col:mean_col + 1],
             scalar2=None, op0=ALU.add), [sq_bb[:], meanbc[:]], [sq_b2[:]])
        p.dma("s", dst_view, sq_b2[0:n_sb, :])

    scaler_quant(s1w1_d.rearrange("(a b) -> a b", b=256), NSB_W12, 0,
                 deq1_d.rearrange("(a b) -> a b", b=256))
    scaler_quant(s1w2_d.rearrange("(a b) -> a b", b=256), NSB_W12, 1,
                 deq2_d.rearrange("(a b) -> a b", b=256))
    scaler_quant(s1w3_d.rearrange("(a b) -> a b", b=256), 88, 2,
                 deq3_d.rearrange("(a b) -> a b", b=256))

    def staircase(wt_ap, wt_t, rows, nblk, rs1_ap, rs1_t, deqt_ap, deqt_t,
                  wrt_ap, wrt_t):
        D = nblk * 64
        p.op("v", lambda v: v.tensor_tensor(
            out=wrt_ap.rearrange("p (a b) -> p a b", b=64),
            in0=wt_ap.rearrange("p (a b) -> p a b", b=64),
            in1=rs1_ap.rearrange("p (a o) -> p a o", o=1)
            .broadcast_to([rows, nblk, 64]),
            op=ALU.mult), [wt_t[:], rs1_t[:]], [wrt_t[:]])
        yield
        accB = ff_f[0:rows, 0:D]
        termB = ff_f[0:rows, 1024:1024 + D]
        p.op("v", lambda v: v.tensor_scalar(out=dq_acc[0:rows, 0:D],
             in0=wrt_ap, scalar1=float(MID[0]),
             scalar2=float(DELTA[0]), op0=ALU.is_gt, op1=ALU.mult),
             [wrt_t[:]], [dq_acc[:]])
        yield
        p.op("v", lambda v: v.tensor_scalar(out=accB,
             in0=wrt_ap, scalar1=float(MID[8]),
             scalar2=float(DELTA[8]), op0=ALU.is_gt, op1=ALU.mult),
             [wrt_t[:]], ["ffB"])
        p.op("v", lambda v: v.tensor_scalar(out=termB,
             in0=wrt_ap, scalar1=float(MID[9]),
             scalar2=float(DELTA[9]), op0=ALU.is_gt, op1=ALU.mult),
             [wrt_t[:]], ["ffT"])
        yield
        for i in range(1, 7):
            p.op("g", lambda g: g.tensor_tensor(out=accB,
                 in0=accB, in1=termB, op=ALU.add), ["ffB", "ffT"], ["ffB"])
            p.op("v", lambda v, j=i: v.tensor_scalar(out=dq_term[0:rows, 0:D],
                 in0=wrt_ap, scalar1=float(MID[j]),
                 scalar2=float(DELTA[j]), op0=ALU.is_gt, op1=ALU.mult),
                 [wrt_t[:]], [dq_term[:]])
            p.op("v", lambda v: v.tensor_tensor(out=dq_acc[0:rows, 0:D],
                 in0=dq_acc[0:rows, 0:D], in1=dq_term[0:rows, 0:D],
                 op=ALU.add), [dq_acc[:], dq_term[:]], [dq_acc[:]])
            if i < 6:
                p.op("v", lambda v, j=9 + i: v.tensor_scalar(out=termB,
                     in0=wrt_ap, scalar1=float(MID[j]),
                     scalar2=float(DELTA[j]), op0=ALU.is_gt, op1=ALU.mult),
                     [wrt_t[:]], ["ffT"])
            yield
        p.op("v", lambda v: v.tensor_scalar(out=dq_term[0:rows, 0:D],
             in0=wrt_ap, scalar1=float(MID[7]),
             scalar2=float(DELTA[7]), op0=ALU.is_gt, op1=ALU.mult),
             [wrt_t[:]], [dq_term[:]])
        p.op("v", lambda v: v.tensor_tensor(out=dq_acc[0:rows, 0:D],
             in0=dq_acc[0:rows, 0:D], in1=dq_term[0:rows, 0:D],
             op=ALU.add), [dq_acc[:], dq_term[:]], [dq_acc[:]])
        yield
        p.op("v", lambda v: v.tensor_tensor(out=dq_acc[0:rows, 0:D],
             in0=dq_acc[0:rows, 0:D], in1=accB, op=ALU.add),
             [dq_acc[:], "ffB"], [dq_acc[:]])
        p.op("v", lambda v: v.scalar_tensor_tensor(
            out=wrt_ap.rearrange("p (a b) -> p a b", b=64),
            in0=dq_acc[0:rows, 0:D].rearrange("p (a b) -> p a b", b=64),
            scalar=1.0,
            in1=deqt_ap.rearrange("p (a o) -> p a o", o=1)
            .broadcast_to([rows, nblk, 64]),
            op0=ALU.subtract, op1=ALU.mult), [dq_acc[:], deqt_t[:]], [wrt_t[:]])


    free("sq_b2", "sq_bb", "sq_b", "sq_i", "sq_a", "sq_rqf",
         "sq_qf", "sq_sabs", "sq_cen", "rowsum", "dq_stage")
    qw_raw = sb("qw_raw", [128, DM], BF16)
    qw_wrt = sb("qw_wrt", [128, DM], BF16)
    qw_rs1 = sb("qw_rs1", [128, NB_W12], F32)
    qw_deqt = sb("qw_deqt", [128, NB_W12], BF16)
    qstg = [sb(f"qstg_{i}", [128, 128], BF16) for i in range(2)]

    def w12_dequant_queue():
        ags = []
        for dram, s1_d, deq_d, ag_in, ag_out_t in (
                (w1s, s1w1_d, deq1_d, agw1_in, agw1_out),
                (w2s, s1w2_d, deq2_d, agw2_in, agw2_out)):
            off = 0
            for i, r in enumerate(W12_RC):
                p.dma("y", qw_raw[0:r, :], dram[off:off + r, :])
                p.dma("y", qw_rs1[0:r, :],
                      s1_d.rearrange("(a b) -> a b", b=NB_W12)[off:off + r, :])
                p.op("v", lambda v, r=r: v.reciprocal(out=qw_rs1[0:r, :],
                     in_=qw_rs1[0:r, :]), [qw_rs1[:]], [qw_rs1[:]])
                p.dma("y", qw_deqt[0:r, :],
                      deq_d.rearrange("(a b) -> a b", b=NB_W12)[off:off + r, :])
                for b0 in (0, 16):
                    yield from staircase(
                        qw_raw[0:r, b0 * 64:(b0 + 16) * 64], qw_raw, r,
                        16, qw_rs1[0:r, b0:b0 + 16], qw_rs1,
                        qw_deqt[0:r, b0:b0 + 16], qw_deqt,
                        qw_wrt[0:r, b0 * 64:(b0 + 16) * 64], qw_wrt)
                    yield
                r0 = sum(W12_RC[:i])
                for dc in range(DM // 128):
                    stg = qstg[dc % 2]
                    emit_tr(qw_wrt[0:r, dc * 128:(dc + 1) * 128], qw_wrt,
                            stg[:, 0:r], stg, r=r, evac="s")
                    p.dma("y", ag_in[dc * 128:(dc + 1) * 128, r0:r0 + r],
                          stg[:, 0:r], wkey=f"qag{dc % 4}",
                          allow_slow_non_contiguous=True)
                    if dc % 2 == 1:
                        yield
                off += r
            ags.append((ag_in, ag_out_t))
        for ag_in, ag_out_t in ags:
            p.op("g", lambda g, ag_in=ag_in, ag_out_t=ag_out_t:
                 g.collective_compute(
                "AllGather", ALU.bypass, replica_groups=[list(range(NCORES))],
                ins=[ag_in], outs=[ag_out_t]),
                [ag_in] + [f"qag{j}" for j in range(4)], [ag_out_t],
                forced_inc=True)
            yield

    w12q = w12_dequant_queue()

    def drain_w12(n):
        for _ in range(n):
            try:
                next(w12q)
            except StopIteration:
                return


    wkv_nat = [sb(f"wkvn_{i}", [128, DM], BF16) for i in range(4)]
    worT = [sb(f"worT_{i}", [128, DM], BF16) for i in range(4)]
    wc2lT = [sb(f"wc2lT_{i}", [128, DL], BF16) for i in range(2)]
    wkvcT = [sb(f"wkvcT_{i}", [128, 2 * DC], BF16) for i in range(2)]
    wqT = [sb(f"wqT_{i}", [128, DC], BF16) for i in range(2)]
    chT_full = [sb(f"chTf_{d}", [128, T], BF16) for d in range(2)]
    chT_own = [sb(f"chTo_{d}", [128, OWN], BF16) for d in range(2)]
    sinF = sb("sinF", [128, T], BF16)
    cosF = sb("cosF", [128, T], BF16)
    sinQ = sb("sinQ", [128, OWN], BF16)
    cosQ = sb("cosQ", [128, OWN], BF16)
    wcmpT = [sb(f"wcmpT_{i}", [128, DC], BF16) for i in range(4)]
    wkvT = [sb(f"wkvT_{i}", [128, DL], BF16) for i in range(16)]
    wld = ff_f
    wld_b = ln_b

    def load_transpose(dram, rows_total, cols, outs, nat_tiles=None):
        r0 = 0
        for i in range((rows_total + 127) // 128):
            r = min(128, rows_total - i * 128)
            p.dma("y", wld[0:r, 0:cols], dram[i * 128:i * 128 + r, :])
            p.op("s", lambda s, r=r, c=cols: s.copy(
                out=wld_b[0:r, 0:c], in_=wld[0:r, 0:c]), [wld[:]], [wld_b[:]])
            if nat_tiles is not None:
                p.op("v", lambda v, i=i, r=r, c=cols: v.tensor_copy(
                    out=nat_tiles[i][:], in_=wld_b[0:r, 0:c]),
                    [wld_b[:]], [nat_tiles[i][:]])
            for c in range(cols // 128):
                emit_tr(wld_b[0:r, c * 128:(c + 1) * 128], wld_b,
                        outs[c][:, r0:r0 + r], outs[c], r=r)
            r0 += r

    load_transpose(wkv, DL, DM, wkvT, nat_tiles=wkv_nat)
    load_transpose(wcmp, DC, DL, wcmpT)
    load_transpose(wq, DC, DC, wqT)
    load_transpose(wkvc, 2 * DC, DC, wkvcT)
    load_transpose(wc2l, DL, DC, wc2lT)
    load_transpose(wor, DM, DL, worT)

    invf = sb("invf", [128, 1], F32)
    ropi = sb("ropi", [128, 1024], I32)
    rop1 = cen_buf
    rop2 = ff_f
    rop3 = xt_buf[0]

    p.op("v", lambda v: v.tensor_scalar(out=invf[:], in0=iota_p[:],
         scalar1=float(-LNTH / 128.0), scalar2=None, op0=ALU.mult),
         [iota_p[:]], [invf[:]])
    p.op("s", lambda s: s.activation(out=invf[:], in_=invf[:], func=AF.Exp),
         [invf[:]], [invf[:]])

    def rope_tables(n, sin_t, cos_t, add_rb):
        if add_rb:
            p.op("v", lambda v: v.tensor_scalar(out=rop1[:, 0:n],
                 in0=tcols[:, 0:n], scalar1=rb_bc[:, 0:1], scalar2=invf[:, 0:1],
                 op0=ALU.add, op1=ALU.mult), [tcols[:], rb_bc[:], invf[:]],
                 [rop1[:]])
        else:
            p.op("v", lambda v: v.tensor_scalar(out=rop1[:, 0:n],
                 in0=tcols[:, 0:n], scalar1=invf[:, 0:1], scalar2=None,
                 op0=ALU.mult), [tcols[:], invf[:]], [rop1[:]])
        p.op("v", lambda v: v.tensor_scalar(out=rop2[:, 0:n], in0=rop1[:, 0:n],
             scalar1=INV2PI, scalar2=None, op0=ALU.mult), [rop1[:]], [rop2[:]])
        for c0 in range(0, n, 1024):
            cw = min(1024, n - c0)
            p.op("v", lambda v, c0=c0, cw=cw: v.tensor_copy(
                out=ropi[:, 0:cw], in_=rop2[:, c0:c0 + cw]),
                [rop2[:]], [ropi[:]])
            p.op("v", lambda v, c0=c0, cw=cw: v.tensor_copy(
                out=rop3[:, c0:c0 + cw], in_=ropi[:, 0:cw]),
                [ropi[:]], [rop3[:]])
        p.op("v", lambda v: v.tensor_tensor(out=rop2[:, 0:n], in0=rop2[:, 0:n],
             in1=rop3[:, 0:n], op=ALU.subtract), [rop2[:], rop3[:]], [rop2[:]])
        p.op("v", lambda v: v.tensor_scalar(out=ln_b[:, 0:n], in0=rop2[:, 0:n],
             scalar1=0.5, scalar2=None, op0=ALU.is_gt), [rop2[:]], [ln_b[:]])
        p.op("v", lambda v: v.tensor_tensor(out=rop3[:, 0:n], in0=rop3[:, 0:n],
             in1=ln_b[:, 0:n], op=ALU.add), [rop3[:], ln_b[:]], [rop3[:]])
        p.op("v", lambda v: v.tensor_scalar(out=ln_b[:, 0:n], in0=rop2[:, 0:n],
             scalar1=-0.5, scalar2=None, op0=ALU.is_lt), [rop2[:]], [ln_b[:]])
        p.op("v", lambda v: v.tensor_tensor(out=rop3[:, 0:n], in0=rop3[:, 0:n],
             in1=ln_b[:, 0:n], op=ALU.subtract), [rop3[:], ln_b[:]], [rop3[:]])
        p.op("v", lambda v: v.scalar_tensor_tensor(out=rop2[:, 0:n],
             in0=rop3[:, 0:n], scalar=-CW1, in1=rop1[:, 0:n],
             op0=ALU.mult, op1=ALU.add), [rop3[:], rop1[:]], [rop2[:]])
        p.op("v", lambda v: v.scalar_tensor_tensor(out=rop2[:, 0:n],
             in0=rop3[:, 0:n], scalar=-CW2, in1=rop2[:, 0:n],
             op0=ALU.mult, op1=ALU.add), [rop3[:], rop2[:]], [rop2[:]])
        p.op("v", lambda v: v.scalar_tensor_tensor(out=rop2[:, 0:n],
             in0=rop3[:, 0:n], scalar=-CW3, in1=rop2[:, 0:n],
             op0=ALU.mult, op1=ALU.add), [rop3[:], rop2[:]], [rop2[:]])
        p.op("v", lambda v: v.tensor_scalar(out=rop2[:, 0:n], in0=rop2[:, 0:n],
             scalar1=PI_LO, scalar2=-PI_LO, op0=ALU.min, op1=ALU.max),
             [rop2[:]], [rop2[:]])
        p.op("s", lambda s: s.activation(out=sin_t[:, 0:n], in_=rop2[:, 0:n],
             func=AF.Sin), [rop2[:]], [sin_t[:]])
        p.op("v", lambda v: v.tensor_scalar(out=rop1[:, 0:n], in0=rop2[:, 0:n],
             scalar1=PH1, scalar2=PH2, op0=ALU.add, op1=ALU.add),
             [rop2[:]], [rop1[:]])
        p.op("v", lambda v: v.tensor_scalar(out=rop2[:, 0:n], in0=rop1[:, 0:n],
             scalar1=PI_F, scalar2=None, op0=ALU.is_gt), [rop1[:]], [rop2[:]])
        p.op("v", lambda v: v.scalar_tensor_tensor(out=rop1[:, 0:n],
             in0=rop2[:, 0:n], scalar=-CW1, in1=rop1[:, 0:n],
             op0=ALU.mult, op1=ALU.add), [rop2[:], rop1[:]], [rop1[:]])
        p.op("v", lambda v: v.scalar_tensor_tensor(out=rop1[:, 0:n],
             in0=rop2[:, 0:n], scalar=-(CW2 + CW3), in1=rop1[:, 0:n],
             op0=ALU.mult, op1=ALU.add), [rop2[:], rop1[:]], [rop1[:]])
        p.op("v", lambda v: v.tensor_scalar(out=rop1[:, 0:n], in0=rop1[:, 0:n],
             scalar1=PI_LO, scalar2=-PI_LO, op0=ALU.min, op1=ALU.max),
             [rop1[:]], [rop1[:]])
        p.op("s", lambda s: s.activation(out=cos_t[:, 0:n], in_=rop1[:, 0:n],
             func=AF.Sin), [rop1[:]], [cos_t[:]])

    rope_tables(T, sinF, cosF, add_rb=False)
    rope_tables(OWN, sinQ, cosQ, add_rb=True)
    free("invf", "ropi")


    def layernorm_to(src_ap, src_t, rows, cols, out_ap, out_t, si=0,
                     act_apply=False):
        mu, ss, rs = ln_mu[si], ln_ss[si], ln_rs[si]
        p.op("s", lambda s: s.activation(out=cen_buf[0:rows, 0:cols]
             .bitcast(F32) if False else cen_buf[0:rows, 0:cols],
             in_=src_ap, func=AF.Square, accum_out=ss[0:rows, :]),
             [src_t[:]], [cen_buf[:], ss[:]])
        p.op("v", lambda v: v.tensor_reduce(out=mu[0:rows, :], in_=src_ap,
             axis=AXX, op=ALU.add), [src_t[:]], [mu[:]])
        p.op("v", lambda v: v.tensor_scalar(out=mu[0:rows, :],
             in0=mu[0:rows, :], scalar1=1.0 / cols, scalar2=None,
             op0=ALU.mult), [mu[:]], [mu[:]])
        p.op("v", lambda v: v.tensor_tensor(out=rs[0:rows, :],
             in0=mu[0:rows, :], in1=mu[0:rows, :], op=ALU.mult),
             [mu[:]], [rs[:]])
        p.op("v", lambda v: v.tensor_scalar(out=ss[0:rows, :],
             in0=ss[0:rows, :], scalar1=1.0 / cols, scalar2=LN_EPS,
             op0=ALU.mult, op1=ALU.add), [ss[:]], [ss[:]])
        p.op("v", lambda v: v.tensor_tensor(out=ss[0:rows, :],
             in0=ss[0:rows, :], in1=rs[0:rows, :], op=ALU.subtract),
             [ss[:], rs[:]], [ss[:]])
        p.op("s", lambda s: s.activation(out=ss[0:rows, :],
             in_=ss[0:rows, :], func=AF.Sqrt), [ss[:]], [ss[:]])
        p.op("v", lambda v: v.reciprocal(out=rs[0:rows, :],
             in_=ss[0:rows, :]), [ss[:]], [rs[:]])
        if act_apply:
            nm = ln_nm[si]
            p.op("v", lambda v: v.tensor_scalar(out=nm[0:rows, :],
                 in0=mu[0:rows, :], scalar1=rs[0:rows, 0:1], scalar2=-1.0,
                 op0=ALU.mult, op1=ALU.mult), [mu[:], rs[:]], [nm[:]])
            p.op("s", lambda s: s.activation(out=out_ap, in_=src_ap,
                 func=AF.Identity, bias=nm[0:rows, 0:1],
                 scale=rs[0:rows, 0:1]),
                 [src_t[:], nm[:], rs[:]], [out_t[:]])
        else:
            p.op("v", lambda v: v.tensor_scalar(out=out_ap,
                 in0=src_ap, scalar1=mu[0:rows, 0:1], scalar2=rs[0:rows, 0:1],
                 op0=ALU.subtract, op1=ALU.mult),
                 [src_t[:], mu[:], rs[:]], [out_t[:]])

    def token_chain(x_dram, ntok, chT_tiles, tag):
        ln0T = [sb(f"{tag}l_{d}", [128, 512], BF16) for d in range(16)]
        kvf = [sb(f"{tag}k_{m}", [128, 512], BF16) for m in range(4)]
        ch_b = sb(f"{tag}c", [128, DC], BF16)
        cf = sb(f"{tag}f", [128, DC], F32)
        for tb in range(ntok // 512):
            for i in range(4):
                ti = tb * 4 + i
                drain_w12(2)
                xt = xt_buf[ti % 2]
                p.dma("y", xt[:], x_dram[ti * 128:(ti + 1) * 128, :])
                layernorm_to(xt[:], xt, 128, DM, ln_b[:], ln_b, si=ti % 2,
                             act_apply=True)
                for d in range(16):
                    emit_tr(ln_b[:, d * 128:(d + 1) * 128], ln_b,
                            ln0T[d][:, i * 128:(i + 1) * 128], ln0T[d])
                    drain_w12(1)
            for m in range(4):
                for d in range(16):
                    p.op("t", lambda t, m=m, d=d: t.matmul(
                        ps_s1[:], lhsT=wkvT[d][:, m * 128:(m + 1) * 128],
                        rhs=ln0T[d][:, 0:512],
                        start=(d == 0), stop=(d == 15)),
                        [wkvT[d][:], ln0T[d][:]], [ps_s1[:]])
                p.op("s", lambda s, m=m: s.copy(out=kvf[m][:], in_=ps_s1[:]),
                     [ps_s1[:]], [kvf[m][:]])
                drain_w12(1)
            for i in range(4):
                ti = tb * 4 + i
                drain_w12(1)
                for m in range(4):
                    p.op("t", lambda t, m=m, i=i: t.matmul(
                        ps_s1[:, 0:DC], lhsT=kvf[m][:, i * 128:(i + 1) * 128],
                        rhs=wcmpT[m][:], start=(m == 0), stop=(m == 3)),
                        [kvf[m][:], wcmpT[m][:]], [ps_s1[:]])
                p.op("v", lambda v: v.tensor_copy(out=cf[:], in_=ps_s1[:, 0:DC]),
                     [ps_s1[:]], [cf[:]])
                layernorm_to(cf[:], cf, 128, DC, cf[:], cf, si=i % 2)
                drain_w12(1)
                p.op("v", lambda v: v.tensor_tensor(out=cf[:], in0=cf[:],
                     in1=gam_bc[:, 0:DC], op=ALU.mult), [cf[:], gam_bc[:]],
                     [cf[:]])
                p.op("v", lambda v: v.tensor_tensor(out=ch_b[:], in0=cf[:],
                     in1=bet_bc[:, 0:DC], op=ALU.add), [cf[:], bet_bc[:]],
                     [ch_b[:]])
                for d in range(2):
                    emit_tr(ch_b[:, d * 128:(d + 1) * 128], ch_b,
                            chT_tiles[d][:, ti * 128:(ti + 1) * 128],
                            chT_tiles[d])
        free(f"{tag}f", f"{tag}c", *(f"{tag}k_{m}" for m in range(3, -1, -1)),
             *(f"{tag}l_{d}" for d in range(15, -1, -1)))

    token_chain(x_own, OWN, chT_own, "Bq")
    for d in range(2):
        p.dma("y", chx_in[d], chT_own[d][:])
    p.op("g", lambda g: g.collective_compute(
        "AllGather", ALU.bypass, replica_groups=[list(range(NCORES))],
        ins=[chx_in], outs=[chx_out]), [chx_in], [chx_out], forced_inc=True)
    ohbc = sb("ohbc", [128, NCORES], F32)
    p.dma("y", ohbc[:], oh_d.broadcast_to([128, NCORES]))
    exst = [ln_b[:, 0:OWN], ln_b[:, OWN:2 * OWN]]
    for j in range(NCORES):
        hh = j % 2
        for d in range(2):
            sk = f"ex{(2 * j + d) % 2}"
            p.dma("y" if (j + d) % 2 == 0 else "s",
                  exst[(2 * j + d) % 2], chx_out[j, d],
                  wkey=sk, rkeys=[chx_out])
            dst = chT_full[d][:, hh * OWN:(hh + 1) * OWN]
            if j < 2:
                p.op("v", lambda v, dst=dst, sk=sk, j=j,
                     src=exst[(2 * j + d) % 2]: v.tensor_scalar(
                    out=dst, in0=src, scalar1=ohbc[:, j:j + 1],
                    scalar2=None, op0=ALU.mult),
                    [sk, ohbc[:]], [chT_full[d][:]])
            else:
                p.op("v", lambda v, dst=dst, sk=sk, j=j,
                     src=exst[(2 * j + d) % 2]: v.scalar_tensor_tensor(
                    out=dst, in0=src, scalar=ohbc[:, j:j + 1],
                    in1=dst, op0=ALU.mult, op1=ALU.add),
                    [sk, ohbc[:], chT_full[d][:]], [chT_full[d][:]])
    free("ohbc")
    drain_w12(4)
    free(*(f"wkvT_{i}" for i in range(16)), *(f"wcmpT_{i}" for i in range(4)))
    kf_b = [sb(f"kfb_{m}", [128, T], BF16) for m in range(2)]
    qf_b = [sb(f"qfb_{m}", [128, OWN], BF16) for m in range(2)]
    v_km = [sb(f"vkm_{i}", [128, 256], BF16) for i in range(16)]

    kpre = [sb(f"kpre_{m}", [128, T], BF16) for m in range(2)]
    vpre = [sb(f"vpre_{m}", [128, T], BF16) for m in range(2)]
    qpre = [sb(f"qpre_{m}", [128, OWN], BF16) for m in range(2)]
    for tb in range(T // 512):
        for m in range(4):
            for d in range(2):
                p.op("t", lambda t, m=m, d=d, tb=tb: t.matmul(
                    ps_s1[:], lhsT=wkvcT[d][:, m * 128:(m + 1) * 128],
                    rhs=chT_full[d][:, tb * 512:(tb + 1) * 512],
                    start=(d == 0), stop=(d == 1)),
                    [wkvcT[d][:], chT_full[d][:]], [ps_s1[:]])
            dstl = kpre if m < 2 else vpre
            p.op("s", lambda s, m=m, tb=tb, dstl=dstl: s.copy(
                out=dstl[m % 2][:, tb * 512:(tb + 1) * 512], in_=ps_s1[:]),
                [ps_s1[:]], [dstl[m % 2][:]])
            drain_w12(2)
    for tb in range(OWN // 512):
        for m in range(2):
            for d in range(2):
                p.op("t", lambda t, m=m, d=d, tb=tb: t.matmul(
                    ps_s1[:], lhsT=wqT[d][:, m * 128:(m + 1) * 128],
                    rhs=chT_own[d][:, tb * 512:(tb + 1) * 512],
                    start=(d == 0), stop=(d == 1)),
                    [wqT[d][:], chT_own[d][:]], [ps_s1[:]])
            p.op("s", lambda s, m=m, tb=tb: s.copy(
                out=qpre[m][:, tb * 512:(tb + 1) * 512], in_=ps_s1[:]),
                [ps_s1[:]], [qpre[m][:]])
            drain_w12(2)

    def rope_apply(pre, post, sin_t, cos_t, n):
        p.op("v", lambda v: v.tensor_tensor(out=cen_buf[:, 0:n], in0=pre[1][:],
             in1=sin_t[:, 0:n], op=ALU.mult), [pre[1][:], sin_t[:]],
             [cen_buf[:]])
        p.op("v", lambda v: v.tensor_tensor(out=post[0][:], in0=pre[0][:],
             in1=cos_t[:, 0:n], op=ALU.mult), [pre[0][:], cos_t[:]],
             [post[0][:]])
        p.op("v", lambda v: v.tensor_tensor(out=post[0][:], in0=post[0][:],
             in1=cen_buf[:, 0:n], op=ALU.subtract),
             [post[0][:], cen_buf[:]], [post[0][:]])
        p.op("v", lambda v: v.tensor_tensor(out=cen_buf[:, 0:n], in0=pre[0][:],
             in1=sin_t[:, 0:n], op=ALU.mult), [pre[0][:], sin_t[:]],
             [cen_buf[:]])
        p.op("v", lambda v: v.tensor_tensor(out=post[1][:], in0=pre[1][:],
             in1=cos_t[:, 0:n], op=ALU.mult), [pre[1][:], cos_t[:]],
             [post[1][:]])
        p.op("v", lambda v: v.tensor_tensor(out=post[1][:], in0=post[1][:],
             in1=cen_buf[:, 0:n], op=ALU.add),
             [post[1][:], cen_buf[:]], [post[1][:]])

    rope_apply(kpre, kf_b, sinF, cosF, T)
    rope_apply(qpre, qf_b, sinQ, cosQ, OWN)
    for i in range(16):
        for d in range(2):
            emit_tr(vpre[d][:, i * 128:(i + 1) * 128], vpre[d],
                    v_km[i][:, d * 128:(d + 1) * 128], v_km[i])
    free("qpre_1", "qpre_0", "vpre_1", "vpre_0", "kpre_1", "kpre_0")

    of_b = [sb(f"of_{d}", [128, OWN], BF16) for d in range(2)]
    scm = cen_buf
    prob_b = [sb(f"prob_b{i}", [128, 2048], BF16) for i in range(2)]
    probT = [sb(f"probT{i}", [128, 2048], BF16) for i in range(2)]

    for t_i in range(8):
        kmax = 1152 + 128 * t_i
        nkb = (kmax + 511) // 512
        for kb in range(nkb):
            nk = min(512, kmax - kb * 512)
            for d in range(2):
                p.op("t", lambda t, t_i=t_i, kb=kb, nk=nk, d=d: t.matmul(
                    ps_sc[:, kb * 512:kb * 512 + nk],
                    lhsT=qf_b[d][:, t_i * 128:(t_i + 1) * 128],
                    rhs=kf_b[d][:, kb * 512:kb * 512 + nk],
                    start=(d == 0), stop=(d == 1)),
                    [qf_b[d][:], kf_b[d][:]], [ps_sc[:]])
        drain_w12(2)
        p.op("v", lambda v, t_i=t_i: v.tensor_scalar(out=rowlim[:],
             in0=iota_p[:], scalar1=float(128 * t_i + 1), scalar2=None,
             op0=ALU.add), [iota_p[:]], [rowlim[:]])
        p.op("v", lambda v: v.tensor_tensor(out=rowlim[:], in0=rowlim[:],
             in1=rb_bc[:], op=ALU.add), [rowlim[:], rb_bc[:]], [rowlim[:]])
        p.op("v", lambda v, kmax=kmax: v.tensor_scalar(out=scm[:, 0:kmax],
             in0=tcols[:, 0:kmax], scalar1=rowlim[:, 0:1], scalar2=NEG,
             op0=ALU.is_ge, op1=ALU.mult), [tcols[:], rowlim[:]], [scm[:]])
        p.op("v", lambda v, kmax=kmax: v.tensor_tensor(out=scm[:, 0:kmax],
             in0=scm[:, 0:kmax], in1=ps_sc[:, 0:kmax], op=ALU.add),
             [scm[:], ps_sc[:]], [scm[:]])
        p.op("v", lambda v, kmax=kmax: v.tensor_reduce(out=rowmax[:],
             in_=scm[:, 0:kmax], axis=AXX, op=ALU.max), [scm[:]], [rowmax[:]])
        p.op("v", lambda v: v.tensor_scalar(out=rowmax[:], in0=rowmax[:],
             scalar1=-SCALE, scalar2=None, op0=ALU.mult), [rowmax[:]],
             [rowmax[:]])
        pb = prob_b[t_i % 2]
        pT = probT[t_i % 2]
        p.op("s", lambda s, kmax=kmax, pb=pb: s.activation(out=pb[:, 0:kmax],
             in_=scm[:, 0:kmax], func=AF.Exp, bias=rowmax[:, 0:1], scale=SCALE,
             accum_out=zsum[:]), [scm[:], rowmax[:]], [pb[:], zsum[:]])
        p.op("v", lambda v: v.reciprocal(out=zsum[:], in_=zsum[:]),
             [zsum[:]], [zsum[:]])
        p.op("s", lambda s, kmax=kmax, pb=pb: s.activation(
             out=pb[:, 0:kmax], in_=pb[:, 0:kmax], func=AF.Copy,
             bias=0.0, scale=zsum[:, 0:1]), [pb[:], zsum[:]], [pb[:]])
        nkc = kmax // 128
        for kc in range(nkc):
            emit_tr(pb[:, kc * 128:(kc + 1) * 128], pb,
                    pT[:, kc * 128:(kc + 1) * 128], pT)
            if kc % 2 == 1:
                drain_w12(1)
        for d in range(2):
            for kc in range(nkc):
                p.op("t", lambda t, d=d, kc=kc, nkc=nkc, pT=pT: t.matmul(
                    ps_s1[:, 0:128], lhsT=v_km[kc][:, d * 128:(d + 1) * 128],
                    rhs=pT[:, kc * 128:(kc + 1) * 128],
                    start=(kc == 0), stop=(kc == nkc - 1)),
                    [v_km[kc][:], pT[:]], [ps_s1[:]])
            p.op("s", lambda s, d=d, t_i=t_i: s.copy(
                out=of_b[d][:, t_i * 128:(t_i + 1) * 128],
                in_=ps_s1[:, 0:128]), [ps_s1[:]], [of_b[d][:]])
    free("probT1", "probT0", "prob_b1", "prob_b0")

    zf_b = [sb(f"zf_{m}", [128, OWN], BF16) for m in range(4)]
    for tb in range(OWN // 512):
        for m in range(4):
            for d in range(2):
                p.op("t", lambda t, m=m, d=d, tb=tb: t.matmul(
                    ps_s1[:], lhsT=wc2lT[d][:, m * 128:(m + 1) * 128],
                    rhs=of_b[d][:, tb * 512:(tb + 1) * 512],
                    start=(d == 0), stop=(d == 1)),
                    [wc2lT[d][:], of_b[d][:]], [ps_s1[:]])
            p.op("s", lambda s, m=m, tb=tb: s.copy(
                out=zf_b[m][:, tb * 512:(tb + 1) * 512], in_=ps_s1[:]),
                [ps_s1[:]], [zf_b[m][:]])

    ao_t = sb("ao_t", [128, DM], F32)
    for ti in range(8):
        for nb in range(4):
            for m in range(4):
                p.op("t", lambda t, ti=ti, nb=nb, m=m: t.matmul(
                    ps_s2[:], lhsT=zf_b[m][:, ti * 128:(ti + 1) * 128],
                    rhs=wkv_nat[m][:, nb * 512:(nb + 1) * 512],
                    start=(m == 0), stop=False),
                    [zf_b[m][:], wkv_nat[m][:]], [ps_s2[:]])
            for m in range(4):
                p.op("t", lambda t, ti=ti, nb=nb, m=m: t.matmul(
                    ps_s2[:], lhsT=zf_b[m][:, ti * 128:(ti + 1) * 128],
                    rhs=worT[m][:, nb * 512:(nb + 1) * 512],
                    start=False, stop=(m == 3)),
                    [zf_b[m][:], worT[m][:]], [ps_s2[:]])
            p.op("v", lambda v, nb=nb: v.tensor_tensor(
                out=ao_t[:, nb * 512:(nb + 1) * 512], in0=ps_s2[:],
                in1=bor_bc[:, nb * 512:(nb + 1) * 512], op=ALU.add),
                [ps_s2[:], bor_bc[:]], [ao_t[:]])
            drain_w12(1)
        xt = xt_buf[ti % 2]
        p.dma("y", xt[:], x_own[ti * 128:(ti + 1) * 128, :])
        layernorm_to(ao_t[:], ao_t, 128, DM, cen_buf[:], cen_buf, si=ti % 2)
        p.op("g", lambda g, xt=xt: g.tensor_tensor(
            out=ao_t[:], in0=cen_buf[:], in1=xt[:], op=ALU.add),
            [cen_buf[:], xt[:]], [ao_t[:]])
        p.dma("y", mid_d[ti * 128:(ti + 1) * 128, :], ao_t[:])
        layernorm_to(ao_t[:], ao_t, 128, DM, ln_b[:], ln_b, si=(ti + 1) % 2,
                     act_apply=True)
        for d in range(16):
            emit_tr(ln_b[:, d * 128:(d + 1) * 128], ln_b,
                    ln1T[d][:, ti * 128:(ti + 1) * 128], ln1T[d])
            if d % 2 == 1:
                drain_w12(1)
    free("ao_t", "zf_3", "zf_2", "zf_1", "zf_0",
         "of_1", "of_0", *(f"vkm_{i}" for i in range(15, -1, -1)),
         "qfb_1", "qfb_0", "kfb_1", "kfb_0",
         "cosQ", "sinQ", "cosF", "sinF", "chTo_1", "chTo_0", "chTf_1",
         "chTf_0",
         *(f"wqT_{i}" for i in range(2)), *(f"wkvcT_{i}" for i in range(2)),
         *(f"wc2lT_{i}" for i in range(2)), *(f"worT_{i}" for i in range(4)),
         *(f"wkvn_{i}" for i in range(4)),
         "qstg_0", "qstg_1", "qw_deqt", "qw_rs1", "qw_wrt", "qw_raw",
         "dq_term", "dq_acc",
         "bor_bc", "bet_bc", "gam_bc", "tcols")

    wstr = [sb(f"wstr_{d}", [128, W12_ROWS * 2], BF16) for d in range(16)]
    h_b = [sb(f"h_{f}", [128, 512], BF16) for f in range(44)]
    dq3stg = [sb(f"dq3stg_{i}", [128, 128], BF16) for i in range(2)]
    dq3sc = sb("dq3sc", [128, 1024], BF16)
    dq3ac = sb("dq3ac", [128, 1024], F32)
    dq3tm = sb("dq3tm", [128, 1024], BF16)

    def stair3(wt_ap, wt_t, rows, nblk, rs1_ap, rs1_t, deqt_ap, deqt_t,
               wrt_ap, wrt_t):
        D = nblk * 64
        p.op("v", lambda v: v.tensor_tensor(
            out=dq3sc[0:rows, 0:D].rearrange("p (a b) -> p a b", b=64),
            in0=wt_ap.rearrange("p (a b) -> p a b", b=64),
            in1=rs1_ap.rearrange("p (a o) -> p a o", o=1)
            .broadcast_to([rows, nblk, 64]),
            op=ALU.mult), [wt_t[:], rs1_t[:]], [dq3sc[:]])
        yield
        accB = ff_f[0:rows, 0:D]
        termB = ff_f[0:rows, 1024:1024 + D]
        p.op("v", lambda v: v.tensor_scalar(out=dq3ac[0:rows, 0:D],
             in0=dq3sc[0:rows, 0:D], scalar1=float(MID[0]),
             scalar2=float(DELTA[0]), op0=ALU.is_gt, op1=ALU.mult),
             [dq3sc[:]], [dq3ac[:]])
        yield
        p.op("v", lambda v: v.tensor_scalar(out=accB,
             in0=dq3sc[0:rows, 0:D], scalar1=float(MID[8]),
             scalar2=float(DELTA[8]), op0=ALU.is_gt, op1=ALU.mult),
             [dq3sc[:]], ["ffB"])
        p.op("v", lambda v: v.tensor_scalar(out=termB,
             in0=dq3sc[0:rows, 0:D], scalar1=float(MID[9]),
             scalar2=float(DELTA[9]), op0=ALU.is_gt, op1=ALU.mult),
             [dq3sc[:]], ["ffT"])
        yield
        for i in range(1, 7):
            p.op("g", lambda g: g.tensor_tensor(out=accB,
                 in0=accB, in1=termB, op=ALU.add), ["ffB", "ffT"], ["ffB"])
            p.op("v", lambda v, j=i: v.tensor_scalar(out=dq3tm[0:rows, 0:D],
                 in0=dq3sc[0:rows, 0:D], scalar1=float(MID[j]),
                 scalar2=float(DELTA[j]), op0=ALU.is_gt, op1=ALU.mult),
                 [dq3sc[:]], [dq3tm[:]])
            p.op("v", lambda v: v.tensor_tensor(out=dq3ac[0:rows, 0:D],
                 in0=dq3ac[0:rows, 0:D], in1=dq3tm[0:rows, 0:D],
                 op=ALU.add), [dq3ac[:], dq3tm[:]], [dq3ac[:]])
            if i < 6:
                p.op("v", lambda v, j=9 + i: v.tensor_scalar(out=termB,
                     in0=dq3sc[0:rows, 0:D], scalar1=float(MID[j]),
                     scalar2=float(DELTA[j]), op0=ALU.is_gt, op1=ALU.mult),
                     [dq3sc[:]], ["ffT"])
            yield
        p.op("v", lambda v: v.tensor_scalar(out=dq3tm[0:rows, 0:D],
             in0=dq3sc[0:rows, 0:D], scalar1=float(MID[7]),
             scalar2=float(DELTA[7]), op0=ALU.is_gt, op1=ALU.mult),
             [dq3sc[:]], [dq3tm[:]])
        p.op("v", lambda v: v.tensor_tensor(out=dq3ac[0:rows, 0:D],
             in0=dq3ac[0:rows, 0:D], in1=dq3tm[0:rows, 0:D],
             op=ALU.add), [dq3ac[:], dq3tm[:]], [dq3ac[:]])
        yield
        p.op("v", lambda v: v.tensor_tensor(out=dq3ac[0:rows, 0:D],
             in0=dq3ac[0:rows, 0:D], in1=accB, op=ALU.add),
             [dq3ac[:], "ffB"], [dq3ac[:]])
        p.op("v", lambda v: v.scalar_tensor_tensor(
            out=wrt_ap.rearrange("p (a b) -> p a b", b=64),
            in0=dq3ac[0:rows, 0:D].rearrange("p (a b) -> p a b", b=64),
            scalar=1.0,
            in1=deqt_ap.rearrange("p (a o) -> p a o", o=1)
            .broadcast_to([rows, nblk, 64]),
            op0=ALU.subtract, op1=ALU.mult), [dq3ac[:], deqt_t[:]], [wrt_t[:]])

    e3t = {}

    def w3_stage(i):
        wrt = e3t[i] = sb(f"e3o_{i}", [128, DFF], BF16)
        raw = sb(f"e3w_{i}", [128, DFF], BF16)
        rs1 = sb(f"e3r_{i}", [128, NB_W3], F32)
        deqt = sb(f"e3q_{i}", [128, NB_W3], BF16)
        p.dma("y", raw[:], w3r[i * 128:(i + 1) * 128, :])
        p.dma("y", rs1[:],
              s1w3_d.rearrange("(a b) -> a b", b=NB_W3)
              [i * 128:(i + 1) * 128, :])
        p.op("v", lambda v, rs1=rs1: v.reciprocal(out=rs1[:], in_=rs1[:]),
             [rs1[:]], [rs1[:]])
        p.dma("y", deqt[:],
              deq3_d.rearrange("(a b) -> a b", b=NB_W3)
              [i * 128:(i + 1) * 128, :])
        for b0 in range(0, NB_W3, 16):
            nb_c = min(16, NB_W3 - b0)
            yield from stair3(
                raw[:, b0 * 64:(b0 + nb_c) * 64], raw, 128, nb_c,
                rs1[:, b0:b0 + nb_c], rs1, deqt[:, b0:b0 + nb_c], deqt,
                wrt[:, b0 * 64:(b0 + nb_c) * 64], wrt)
            yield

    def w3_transposes(i, ag_in):
        wrt = e3t[i]
        for fc in range(DFF // 128):
            stg = dq3stg[fc % 2]
            emit_tr(wrt[:, fc * 128:(fc + 1) * 128], wrt, stg[:], stg,
                    evac="s")
            p.dma("y", ag_in[fc * 128:(fc + 1) * 128, :], stg[:],
                  allow_slow_non_contiguous=True)
            if fc % 4 == 3:
                yield

    def w3_ag_repack(ag_in, ag_out, i):
        p.op("g", lambda g: g.collective_compute(
            "AllGather", ALU.bypass, replica_groups=[list(range(NCORES))],
            ins=[ag_in], outs=[ag_out]), [ag_in], [ag_out],
            forced_inc=True)
        yield
        for k in range(NCORES):
            p.dma("y" if k % 2 == 0 else "s",
                  w3rm[:, k * 256 + i * 128:k * 256 + (i + 1) * 128],
                  ag_out[k, :, :],
                  wkey=f"w3rm#{k}", rkeys=[ag_out],
                  allow_slow_non_contiguous=True)
        yield

    def w3_dequant_queue():
        yield from w3_stage(0)
        yield from w3_transposes(0, agw3_inA)
        free("e3q_0", "e3r_0", "e3w_0", "e3o_0")
        yield
        yield from w3_stage(1)
        yield from w3_ag_repack(agw3_inA, agw3_outA, 0)
        yield from w3_transposes(1, agw3_inB)
        free("e3q_1", "e3r_1", "e3w_1", "e3o_1")
        yield
        yield from w3_ag_repack(agw3_inB, agw3_outB, 1)

    w3q = w3_dequant_queue()
    w3str = [sb(f"w3str_{i}", [128, 1024], BF16) for i in range(2)]
    ff4 = [ff_f, None, None, None]

    def drain_q(n):
        for _ in range(n):
            try:
                next(w3q)
            except StopIteration:
                return

    def ab_phase(ag_out, half, is_b, ps_abs, drain=0):
        for r in range(NCORES):
            bp = r % 2
            if drain:
                drain_q(drain)
            for d in range(16):
                p.dma("y" if d % 2 == 0 else "s",
                      wstr[d][:, bp * W12_ROWS:(bp + 1) * W12_ROWS],
                      ag_out[r, d * 128:(d + 1) * 128, :])
            chunks = []
            g = W12_ROWS * r
            rem = W12_ROWS
            while rem:
                c = min(128 - g % 128, rem)
                chunks.append((g - W12_ROWS * r, c))
                g += c
                rem -= c
            for ci, (fo, fw) in enumerate(chunks):
                ps_ab = ps_abs[ci % 2]
                if drain:
                    drain_q(1)
                for d in range(16):
                    p.op("t", lambda t, d=d, bp=bp, fo=fo, fw=fw, half=half,
                         ps_ab=ps_ab: t.matmul(
                        ps_ab[0:fw, :],
                        lhsT=wstr[d][:, bp * W12_ROWS + fo:
                                     bp * W12_ROWS + fo + fw],
                        rhs=ln1T[d][:, half * 512:(half + 1) * 512],
                        start=(d == 0), stop=(d == 15)),
                        [wstr[d][:], ln1T[d][:]], [ps_ab[:]])
                gpos = r * W12_ROWS + fo
                hi, ho = gpos // 128, gpos % 128
                assert ho + fw <= 128, (r, fo, fw)
                if not is_b:
                    p.op("s", lambda s, hi=hi, ho=ho, fw=fw, ps_ab=ps_ab:
                         s.activation(out=h_b[hi][ho:ho + fw, :],
                                      in_=ps_ab[0:fw, :], func=AF.Silu),
                         [ps_ab[:]], [h_b[hi][:]])
                else:
                    p.op("v", lambda v, hi=hi, ho=ho, fw=fw, ps_ab=ps_ab:
                         v.tensor_tensor(out=h_b[hi][ho:ho + fw, :],
                                         in0=h_b[hi][ho:ho + fw, :],
                                         in1=ps_ab[0:fw, :], op=ALU.mult),
                         [h_b[hi][:], ps_ab[:]], [h_b[hi][:]])

    def w3_phase(half, ps_fs):
        for dmh in range(2):
            for fc in range(44):
                buf = w3str[fc % 2]
                p.dma("y" if fc % 2 == 0 else "s", buf[:],
                      w3rm[fc * 128:(fc + 1) * 128,
                           dmh * 1024:(dmh + 1) * 1024],
                      wkey=f"w3s{fc % 2}",
                      rkeys=[f"w3rm#{j}" for j in range(NCORES)])
                for tl in range(4):
                    for nb in range(2):
                        p.op("t", lambda t, fc=fc, nb=nb, tl=tl, buf=buf:
                             t.matmul(
                            ps_fs[tl][:, nb * 512:(nb + 1) * 512],
                            lhsT=h_b[fc][:, tl * 128:(tl + 1) * 128],
                            rhs=buf[:, nb * 512:(nb + 1) * 512],
                            start=(fc == 0), stop=(fc == 43)),
                            [h_b[fc][:], f"w3s{fc % 2}"],
                            [ps_fs[tl][:]])
            for tl in range(4):
                p.op("s", lambda s, tl=tl, dmh=dmh: s.copy(
                    out=ff4[tl][:, dmh * 1024:(dmh + 1) * 1024],
                    in_=ps_fs[tl][:]), [ps_fs[tl][:]], [ff4[tl][:]])
        for tl in range(4):
            tc = half * 4 + tl
            xt = xt_buf[tc % 2]
            p.dma("y", xt[:], mid_d[tc * 128:(tc + 1) * 128, :])
            layernorm_to(ff4[tl][:], ff4[tl], 128, DM, cen_buf[:], cen_buf,
                         si=tc % 2)
            p.op("g", lambda g, xt=xt, tl=tl: g.tensor_tensor(
                out=ff4[tl][:], in0=cen_buf[:], in1=xt[:], op=ALU.add),
                [cen_buf[:], xt[:]], [ff4[tl][:]])
            p.dma("y", out[tc * 128:(tc + 1) * 128, :], ff4[tl][:])

    drain_w12(10 ** 6)
    free_ps("ps_sc", "ps_s2", "ps_s1")
    for half in range(2):
        ps_abs = [ps(f"ps_ab{half}_0", [128, 512], F32),
                  ps(f"ps_ab{half}_1", [128, 512], F32)]
        ab_phase(agw1_out, half, is_b=False, ps_abs=ps_abs,
                 drain=3 if half == 0 else 0)
        ab_phase(agw2_out, half, is_b=True, ps_abs=ps_abs,
                 drain=3 if half == 0 else 0)
        if half == 0:
            drain_q(10 ** 6)
            for j in range(1, 4):
                ff4[j] = sb(f"ff4_{j}", [128, DM], F32)
            free_ps(f"ps_ab{half}_1", f"ps_ab{half}_0", "ps_tr2", "ps_tr")
        else:
            free_ps(f"ps_ab{half}_1", f"ps_ab{half}_0")
        ps_fs = [ps(f"ps_f{half}_{j}", [128, 1024], F32) for j in range(4)]
        w3_phase(half, ps_fs)
        free_ps(*(f"ps_f{half}_{j}" for j in range(3, -1, -1)))

    sems = ExitStack()
    block = nc.Block()
    block.__enter__()
    p.finalize(block, sems)
    block.__exit__(None, None, None)
    for n in list(cms):
        if n.startswith("ps_"):
            cms.pop(n).__exit__(None, None, None)
    for n in reversed(stack):
        cms.pop(n).__exit__(None, None, None)
    for cm in list(cms.values()):
        cm.__exit__(None, None, None)
    sems.close()
    return nc


def kernel(L0_attn_x, L0_attn_W_to_kv, L0_attn_W_compress, L0_attn_ln_gamma,
           L0_attn_ln_beta, L0_attn_W_q_comp, L0_attn_W_kv_comp,
           L0_attn_W_comp2lat, L0_attn_W_out_res, L0_attn_b_out_res,
           L0_ffn_weight1, L0_ffn_weight2, L0_ffn_weight3):
    x = np.ascontiguousarray(np.asarray(L0_attn_x, dtype=np.float32))
    wkv_h = np.ascontiguousarray(np.asarray(L0_attn_W_to_kv, dtype=np.float32))
    wcmp_h = np.ascontiguousarray(np.asarray(L0_attn_W_compress, dtype=np.float32))
    gam_h = np.asarray(L0_attn_ln_gamma, dtype=np.float32).reshape(1, DC)
    bet_h = np.asarray(L0_attn_ln_beta, dtype=np.float32).reshape(1, DC)
    wq_h = np.ascontiguousarray(np.asarray(L0_attn_W_q_comp, dtype=np.float32))
    wkvc_h = np.ascontiguousarray(np.asarray(L0_attn_W_kv_comp, dtype=np.float32))
    wc2l_h = np.ascontiguousarray(np.asarray(L0_attn_W_comp2lat, dtype=np.float32))
    wor_h = np.ascontiguousarray(np.asarray(L0_attn_W_out_res, dtype=np.float32))
    bor_h = np.asarray(L0_attn_b_out_res, dtype=np.float32).reshape(1, DM)
    w1 = np.asarray(L0_ffn_weight1)
    w2 = np.asarray(L0_ffn_weight2)
    w3 = np.asarray(L0_ffn_weight3)

    nc = build_program()
    in_maps = []
    for c in range(NCORES):
        b, h = c // 2, c % 2
        in_maps.append({
            "x_full": x[b],
            "x_own": np.ascontiguousarray(x[b, h * OWN:(h + 1) * OWN]),
            "rowbase": np.array([[h * OWN]], dtype=np.float32),
            "oh": (np.arange(NCORES)[None, :] // 2 == b)
            .astype(np.float32),
            "wkv": wkv_h, "wcmp": wcmp_h, "gam": gam_h, "bet": bet_h,
            "wq": wq_h, "wkvc": wkvc_h, "wc2l": wc2l_h, "wor": wor_h,
            "bor": bor_h,
            "w1s": np.ascontiguousarray(w1[c * W12_ROWS:(c + 1) * W12_ROWS]),
            "w2s": np.ascontiguousarray(w2[c * W12_ROWS:(c + 1) * W12_ROWS]),
            "w3r": np.ascontiguousarray(w3[c * W3_ROWS:(c + 1) * W3_ROWS]),
        })
    res = run_bass_kernel_spmd(nc, in_maps, core_ids=list(range(NCORES)))
    outp = np.empty((B, T, DM), dtype=np.float32)
    for c in range(NCORES):
        b, h = c // 2, c % 2
        outp[b, h * OWN:(h + 1) * OWN] = res.results[c]["out"]
    return outp
